# version 19
# baseline (speedup 1.0000x reference)
import contextlib
import os
os.environ.setdefault("TRN_ALLOW_ICI_ALL_TO_ALL", "1")
import numpy as np
import ml_dtypes
import concourse.bass as bass
import concourse.mybir as mybir
import bass_rust
from concourse.bass_utils import run_bass_kernel_spmd

F32 = mybir.dt.float32
BF16 = mybir.dt.bfloat16
AF = mybir.ActivationFunctionType
ALU = mybir.AluOpType
AX = mybir.AxisListType.X

NCORES = 8
BATCH, L, DM = 2, 8192, 1024
DI, NST, KCONV = 2048, 16, 4
TLOC = 2048
HALO = 128
TEXT = TLOC + HALO
DSL = DI // NCORES
C23 = 1.5 * 2.0 ** 23
C52 = 1.5 * 2.0 ** 52
EPS_LN = 1e-5
EPS_Q = 1e-5
QB = 127.0
INV_QB = 1.0 / 127.0
BF = ml_dtypes.bfloat16
DBG_X1 = False
DBG_Y2 = False
DBG_STOP = False
DBG_STOP0 = False


class Prog:
    ENGS = ("sync", "scalar", "vector", "tensor", "gpsimd")

    def __init__(self, nc):
        self.nc = nc
        self.q = {e: [] for e in self.ENGS}
        self.sem = {}
        self.cnt = {}
        self.stack = contextlib.ExitStack()
        self.phases = []
        self.prefix = ""
        self.dirty = {}

    def _n(self, name):
        return self.prefix + name

    def new_sem(self, name):
        n = self._n(name)
        self.sem[n] = self.stack.enter_context(self.nc.semaphore(n))
        self.cnt[n] = 0
        return n

    def total(self, name):
        n = self._n(name)
        return (n, self.cnt[n])

    def sbuf(self, name, shape, dt):
        st = self.phases[-1] if self.phases else self.stack
        return st.enter_context(
            self.nc.sbuf_tensor(self._n(name), list(shape), dt))

    def psum(self, name, shape, dt=F32):
        return self.stack.enter_context(self.nc.psum_tensor(name, list(shape), dt))

    def phase_begin(self):
        self.phases.append(contextlib.ExitStack())

    def phase_end(self):
        self.phases.pop().close()

    def wait(self, eng, tok):
        if tok is None:
            return
        sem, val = tok
        if val <= 0:
            return
        h = self.sem[sem]
        self.q[eng].append(lambda e, h=h, v=val: e.wait_ge(h, v))

    def dma(self, eng, out, in_, sem):
        sem = self._n(sem)
        h = self.sem[sem]

        def f(e, out=out, in_=in_, h=h):
            e.dma_start(out=out, in_=in_).then_inc(h, 16)

        self.q[eng].append(f)
        self.cnt[sem] += 16
        return (sem, self.cnt[sem])

    WRITE_KW = ("out", "accum_out", "ap")

    def op(self, eng, name, sem=None, **kw):
        sem = self._n(sem) if sem is not None else None
        h = self.sem[sem] if sem is not None else None
        late = eng in ("vector", "gpsimd", "scalar")
        pre_drain = False
        if late:
            dw, dr = self.dirty.setdefault(eng, (set(), set()))
            rd, wr = set(), set()
            for k, v in kw.items():
                if isinstance(v, bass_rust.AP):
                    (wr if k in self.WRITE_KW else rd).add(v.name)
            if ((rd | wr) & dw) or (wr & dr):
                pre_drain = True
                dw.clear()
                dr.clear()
            if h is not None:
                dw.clear()
                dr.clear()
            else:
                dw |= wr
                dr |= rd

        def f(e, name=name, kw=kw, h=h, late=late, pre=pre_drain):
            if pre:
                e.drain()
            i = getattr(e, name)(**kw)
            if late:
                if h is not None:
                    e.drain()
                    e.sem_inc(h, 1)
            elif h is not None:
                i.then_inc(h, 1)

        self.q[eng].append(f)
        if sem is not None:
            self.cnt[sem] += 1
            return (sem, self.cnt[sem])
        return None

    def build_block(self):
        nc = self.nc
        with nc.Block() as block:
            @block.sync
            def _(eng):
                for f in self.q["sync"]:
                    f(eng)

            @block.scalar
            def _(eng):
                for f in self.q["scalar"]:
                    f(eng)

            @block.vector
            def _(eng):
                for f in self.q["vector"]:
                    f(eng)

            @block.tensor
            def _(eng):
                for f in self.q["tensor"]:
                    f(eng)

            @block.gpsimd
            def _(eng):
                for f in self.q["gpsimd"]:
                    f(eng)
        self.stack.close()


class PP:
    def __init__(self, p, n=2):
        self.p, self.n, self.t = p, n, [None] * n

    def guard(self, eng, i):
        self.p.wait(eng, self.t[i % self.n])

    def done(self, i, tok):
        self.t[i % self.n] = tok



def emit_P1(p, io):
    (xT, W_inT, W_dtT, cw, cb, bdt, dtb, u_out, g_out, dt_out, dtu_out, pm) = (
        io["xT"], io["W_inT"], io["W_dtT"], io["cw"], io["cb"], io["bdt"],
        io["dtb"], io["u_out"], io["g_out"], io["dt_out"], io["dtu_out"],
        io["pm"])
    p.phase_begin()
    xbf = [p.sbuf(f"xbf{k}", [128, TEXT], BF16) for k in range(8)]
    wreg = p.sbuf("wreg", [128, 32768], BF16)
    stg = [p.sbuf(f"stg{i}", [128, TEXT], F32) for i in range(2)]
    xzf = [p.sbuf(f"xzf{i}", [128, TEXT], F32) for i in range(2)]
    cacc = [p.sbuf(f"cacc{i}", [128, TLOC], F32) for i in range(2)]
    ubfc = [p.sbuf(f"ubfc{i}", [128, 16 * 512], BF16) for i in range(2)]
    urel = [p.sbuf(f"urel{i}", [128, 512], F32) for i in range(2)]
    dt_t = [p.sbuf(f"dtt{i}", [128, 512], F32) for i in range(2)]
    dtu_t = [p.sbuf(f"dtut{i}", [128, 512], F32) for i in range(2)]
    cw_sb = p.sbuf("cw_sb", [128, 16 * KCONV], F32)
    cb_sb = p.sbuf("cb_sb", [128, 16], F32)
    bdt_sb = p.sbuf("bdt_sb", [128, 16], F32)
    dtb_sb = p.sbuf("dtb_sb", [128, 16], F32)
    spb_sb = p.sbuf("spb_sb", [128, 16], F32)

    for s in ("sCst", "sT", "sV", "sS", "sStg0", "sStg1", "sUr0", "sUr1",
              "sU0", "sU1", "sG0", "sG1", "sDt0", "sDt1", "sDu0", "sDu1"):
        p.new_sem(s)

    stg_pp = PP(p, 2)
    stg_n = [0]

    def staged_cast(dram_slice, width, cast_out):
        i = stg_n[0]
        stg_pp.guard("sync", i)
        t = p.dma("sync", stg[i % 2][:, 0:width], dram_slice,
                  "sStg0" if i % 2 == 0 else "sStg1")
        p.wait("vector", t)
        tok = p.op("vector", "tensor_copy", sem="sV",
                   out=cast_out, in_=stg[i % 2][:, 0:width])
        stg_pp.done(i, tok)
        stg_n[0] += 1
        return tok

    for k in range(8):
        staged_cast(xT[128 * k:128 * (k + 1), :], TEXT, xbf[k][:])
    win_tok = None
    for k in range(8):
        for hh in range(2):
            win_tok = staged_cast(
                W_inT[128 * k:128 * (k + 1), 2048 * hh:2048 * (hh + 1)], 2048,
                wreg[:, 4096 * k + 2048 * hh:4096 * k + 2048 * (hh + 1)])

    p.dma("sync", cw_sb[:], cw, "sCst")
    p.dma("sync", cb_sb[:], cb, "sCst")
    p.dma("sync", bdt_sb[:], bdt, "sCst")
    consts_tok = p.dma("sync", dtb_sb[:], dtb, "sCst")
    p.wait("vector", consts_tok)
    spb_tok = p.op("vector", "tensor_tensor", sem="sV",
                   out=spb_sb[:], in0=bdt_sb[:], in1=dtb_sb[:], op=ALU.add)

    CH_X = [(0, 512), (512, 512), (1024, 512), (1536, 512), (2048, 128)]
    CH_G = [(128, 512), (640, 512), (1152, 512), (1664, 512)]

    psum_free = {}
    gseq = 0
    gt_pp = PP(p, 2)
    cacc_pp = PP(p, 2)
    last_xz_mm = None
    gate_gi = 0

    for m in range(32):
        chunks = CH_X if m < 16 else CH_G
        for ci, (c0, cn) in enumerate(chunks):
            if gseq == 0:
                p.wait("tensor", win_tok)
            p.wait("tensor", psum_free.get(gseq - 2))
            tok = None
            for k in range(8):
                tok = p.op("tensor", "matmul", sem="sT" if k == 7 else None,
                           out=pm[gseq % 2][:, 0:cn],
                           lhsT=wreg[:, 4096 * k + 128 * m:4096 * k + 128 * m + 128],
                           rhs=xbf[k][:, c0:c0 + cn],
                           start=(k == 0), stop=(k == 7))
            last_xz_mm = tok
            p.wait("vector", tok)
            if m < 16:
                psum_free[gseq] = p.op("vector", "tensor_copy", sem="sV",
                                       out=xzf[m % 2][:, c0:c0 + cn],
                                       in_=pm[gseq % 2][:, 0:cn])
            else:
                gt_pp.guard("vector", gate_gi)
                ev = p.op("vector", "tensor_copy", sem="sV",
                          out=dtu_t[gate_gi % 2][:, 0:cn],
                          in_=pm[gseq % 2][:, 0:cn])
                psum_free[gseq] = ev
                p.wait("gpsimd", ev)
                st = p.dma("gpsimd",
                           g_out[128 * (m - 16):128 * (m - 15),
                                 c0 - HALO:c0 - HALO + cn],
                           dtu_t[gate_gi % 2][:, 0:cn],
                           "sG0" if gate_gi % 2 == 0 else "sG1")
                gt_pp.done(gate_gi, st)
                gate_gi += 1
            gseq += 1

        if m < 16:
            cacc_pp.guard("vector", m)
            if m == 0:
                p.wait("vector", consts_tok)
            p.op("vector", "tensor_scalar",
                 out=cacc[m % 2][:], in0=xzf[m % 2][:, 125:125 + TLOC],
                 scalar1=cw_sb[:, 4 * m:4 * m + 1], scalar2=None, op0=ALU.mult)
            conv_tok = None
            for j in (1, 2, 3):
                conv_tok = p.op("vector", "scalar_tensor_tensor",
                                sem="sV" if j == 3 else None,
                                out=cacc[m % 2][:],
                                in0=xzf[m % 2][:, 125 + j:125 + j + TLOC],
                                scalar=cw_sb[:, 4 * m + j:4 * m + j + 1],
                                in1=cacc[m % 2][:], op0=ALU.mult, op1=ALU.add)
            p.wait("scalar", conv_tok)
            silu_tok = p.op("scalar", "activation", sem="sS",
                            out=cacc[m % 2][:], in_=cacc[m % 2][:],
                            func=AF.Silu, bias=cb_sb[:, m:m + 1], scale=1.0)
            p.wait("gpsimd", silu_tok)
            ust = p.dma("gpsimd", u_out[128 * m:128 * (m + 1), :],
                        cacc[m % 2][:], "sU0" if m % 2 == 0 else "sU1")
            cacc_pp.done(m, ust)

    u_done = [p.total("sU0"), p.total("sU1")]

    wdt_ready = None
    for k in range(16):
        i = stg_n[0]
        stg_pp.guard("sync", i)
        t = p.dma("sync", stg[i % 2][:, 0:2048],
                  W_dtT[128 * k:128 * (k + 1), :],
                  "sStg0" if i % 2 == 0 else "sStg1")
        p.wait("vector", t)
        if k == 0:
            p.wait("vector", last_xz_mm)
        wdt_ready = p.op("vector", "tensor_copy", sem="sV",
                         out=wreg[:, 2048 * k:2048 * (k + 1)],
                         in_=stg[i % 2][:, 0:2048])
        stg_pp.done(i, wdt_ready)
        stg_n[0] += 1

    dtut_pp0 = [gt_pp.t[0], gt_pp.t[1]]
    ubfc_pp = PP(p, 2)
    urel_pp = PP(p, 2)
    dtt_pp = PP(p, 2)
    dtut_pp = PP(p, 2)
    dtut_pp.t = dtut_pp0
    dtu_guard = {}
    di = 0
    for n in range(4):
        c0 = 512 * n
        if n == 0:
            p.wait("sync", u_done[0])
            p.wait("sync", u_done[1])
        ub_tok = None
        for k in range(16):
            i = stg_n[0]
            stg_pp.guard("sync", i)
            t = p.dma("sync", stg[i % 2][:, 0:512],
                      u_out[128 * k:128 * (k + 1), c0:c0 + 512],
                      "sStg0" if i % 2 == 0 else "sStg1")
            p.wait("vector", t)
            if k == 0:
                ubfc_pp.guard("vector", n)
            ub_tok = p.op("vector", "tensor_copy", sem="sV",
                          out=ubfc[n % 2][:, 512 * k:512 * (k + 1)],
                          in_=stg[i % 2][:, 0:512])
            stg_pp.done(i, ub_tok)
            stg_n[0] += 1
        for m in range(16):
            urel_pp.guard("sync", di)
            ur_tok = p.dma("sync", urel[di % 2][:],
                           u_out[128 * m:128 * (m + 1), c0:c0 + 512],
                           "sUr0" if di % 2 == 0 else "sUr1")
            if di == 0:
                p.wait("tensor", wdt_ready)
            if m == 0:
                p.wait("tensor", ub_tok)
            p.wait("tensor", psum_free.get(gseq - 2))
            tok = None
            for k in range(16):
                tok = p.op("tensor", "matmul", sem="sT" if k == 15 else None,
                           out=pm[gseq % 2][:, 0:512],
                           lhsT=wreg[:, 2048 * k + 128 * m:2048 * k + 128 * m + 128],
                           rhs=ubfc[n % 2][:, 512 * k:512 * (k + 1)],
                           start=(k == 0), stop=(k == 15))
            if m == 15:
                ubfc_pp.done(n, tok)
            p.wait("scalar", tok)
            if di == 0:
                p.wait("scalar", spb_tok)
            dtt_pp.guard("scalar", di)
            p.wait("scalar", dtu_guard.get(di - 2))
            exp_tok = p.op("scalar", "activation", sem="sS",
                           out=dt_t[di % 2][:], in_=pm[gseq % 2][:, 0:512],
                           func=AF.Exp, bias=spb_sb[:, m:m + 1], scale=1.0)
            psum_free[gseq] = exp_tok
            ln_tok = p.op("scalar", "activation", sem="sS",
                          out=dt_t[di % 2][:], in_=dt_t[di % 2][:],
                          func=AF.Ln, bias=1.0, scale=1.0)
            p.wait("vector", ln_tok)
            p.wait("vector", ur_tok)
            dtut_pp.guard("vector", di)
            dtu_tok = p.op("vector", "tensor_tensor", sem="sV",
                           out=dtu_t[di % 2][:], in0=dt_t[di % 2][:],
                           in1=urel[di % 2][:], op=ALU.mult)
            dtu_guard[di] = dtu_tok
            urel_pp.done(di, dtu_tok)
            p.wait("gpsimd", ln_tok)
            st1 = p.dma("gpsimd",
                        dt_out[n, m // 2, 128 * (m % 2):128 * (m % 2) + 128, :],
                        dt_t[di % 2][:], "sDt0" if di % 2 == 0 else "sDt1")
            p.wait("gpsimd", dtu_tok)
            st2 = p.dma("gpsimd",
                        dtu_out[n, m // 2, 128 * (m % 2):128 * (m % 2) + 128, :],
                        dtu_t[di % 2][:], "sDu0" if di % 2 == 0 else "sDu1")
            dtt_pp.done(di, st1)
            dtut_pp.done(di, st2)
            gseq += 1
            di += 1

    p.phase_end()
    return {
        "stores": [p.total(nm) for nm in
                   ("sU0", "sU1", "sG0", "sG1", "sDt0", "sDt1", "sDu0", "sDu1")],
        "dtdu": [p.total(nm) for nm in ("sDt0", "sDt1", "sDu0", "sDu1")],
        "ug": [p.total(nm) for nm in ("sU0", "sU1", "sG0", "sG1")],
        "psum_free": [psum_free.get(gseq - 2), psum_free.get(gseq - 1)],
    }


def build_P1():
    nc = bass.Bass("TRN2", target_bir_lowering=False)
    p = Prog(nc)
    io = {
        "xT": nc.dram_tensor("xT", [DM, TEXT], F32, kind="ExternalInput").ap(),
        "W_inT": nc.dram_tensor("W_inT", [DM, 2 * DI], F32, kind="ExternalInput").ap(),
        "W_dtT": nc.dram_tensor("W_dtT", [DI, DI], F32, kind="ExternalInput").ap(),
        "cw": nc.dram_tensor("cw", [128, 16 * KCONV], F32, kind="ExternalInput").ap(),
        "cb": nc.dram_tensor("cb", [128, 16], F32, kind="ExternalInput").ap(),
        "bdt": nc.dram_tensor("bdt", [128, 16], F32, kind="ExternalInput").ap(),
        "dtb": nc.dram_tensor("dtb", [128, 16], F32, kind="ExternalInput").ap(),
        "u_out": nc.dram_tensor("u_out", [DI, TLOC], F32, kind="ExternalOutput").ap(),
        "g_out": nc.dram_tensor("g_out", [DI, TLOC], F32, kind="ExternalOutput").ap(),
        "dt_out": nc.dram_tensor("dt_out", [4, 8, DSL, 512], F32, kind="ExternalOutput").ap(),
        "dtu_out": nc.dram_tensor("dtu_out", [4, 8, DSL, 512], F32, kind="ExternalOutput").ap(),
        "pm": [nc.psum_tensor(f"pm{i}", [128, 512], F32).__enter__() for i in range(2)],
    }
    r = emit_P1(p, io)
    for t in r["stores"]:
        p.wait("sync", t)
    p.build_block()
    return nc



def emit_P2(p, io):
    (dta_o, dtua_o, alog, bp, cp, idf, ya_in, pY) = (
        io["dta_o"], io["dtua_o"], io["alog"], io["bp"], io["cp"], io["idf"],
        io["ya_in"], io["pY"])
    TC = 2048
    NTC = L // TC
    p.phase_begin()
    al_sb = p.sbuf("al_sb", [128, 2 * NST], F32)
    a_sb = p.sbuf("a_sb", [128, 2 * NST], F32)
    ax_sb = p.sbuf("ax_sb", [128, 2 * NST], F32)
    b_sb = p.sbuf("b_sb", [128, 2 * NST], F32)
    c_sb = p.sbuf("c_sb", [128, 2 * NST], F32)
    g_sb = p.sbuf("g_sb", [128, 2 * NST], F32)
    id_sb = p.sbuf("id_sb", [128, 128], F32)
    diagG = p.sbuf("diagG", [128, 2 * NST * 128], BF16)
    dt_c = [p.sbuf(f"dtc{i}", [128, TC], F32) for i in range(2)]
    dtu_c = [p.sbuf(f"dtuc{i}", [128, TC], F32) for i in range(2)]
    dA = [p.sbuf(f"dA{i}", [128, TC], F32) for i in range(4)]
    dA.append(p.psum("dA4", [128, TC]))
    ysb = [p.sbuf(f"ysb{i}", [128, 512], F32) for i in range(2)]
    s_n = [[p.sbuf(f"s_{n}_{i}", [128, TC], BF16) for i in range(2)]
           for n in range(NST)]

    for s in ("sCst", "sT", "sV", "sS", "sIn0", "sIn1", "sY0", "sY1"):
        p.new_sem(s)

    p.dma("sync", al_sb[:], alog, "sCst")
    p.dma("sync", b_sb[:], bp, "sCst")
    p.dma("sync", c_sb[:], cp, "sCst")
    t3 = p.dma("sync", id_sb[:], idf, "sCst")
    t0 = t3
    t2 = t3
    p.wait("scalar", t0)
    tok = p.op("scalar", "activation", sem="sS", out=ax_sb[:], in_=al_sb[:],
               func=AF.Exp, bias=0.0, scale=1.0)
    p.wait("vector", tok)
    a_tok = p.op("vector", "tensor_scalar", sem="sV", out=a_sb[:], in0=ax_sb[:],
                 scalar1=-1.0, scalar2=None, op0=ALU.mult)
    p.wait("vector", t2)
    p.op("vector", "tensor_tensor", out=g_sb[:], in0=b_sb[:], in1=c_sb[:],
         op=ALU.mult)
    p.wait("vector", t3)
    dg_tok = None
    for col in range(2 * NST):
        dg_tok = p.op("vector", "tensor_scalar", sem="sV",
                      out=diagG[:, 128 * col:128 * (col + 1)], in0=id_sb[:],
                      scalar1=g_sb[:, col:col + 1], scalar2=None, op0=ALU.mult)

    in_pp = PP(p, 2)
    scan_toks = {}
    sall_pp = PP(p, 2)
    pY_pp = PP(p, 2)
    ysb_pp = PP(p, 2)

    it = 0
    for ss in range(4):
        b, dblk = ss // 2, ss % 2
        r0 = 128 * dblk
        for tc in range(NTC):
            c0 = TC * tc
            in_pp.guard("sync", it)
            if it == 0:
                if io.get("entry") is not None:
                    p.wait("sync", io["entry"])
                if io.get("entry2") is not None:
                    p.wait("sync", io["entry2"])
            shard = 4 * b + tc
            sem_in = "sIn0" if it % 2 == 0 else "sIn1"
            for nn in range(4):
                p.dma("sync", dt_c[it % 2][:, 512 * nn:512 * (nn + 1)],
                      dta_o[nn, :, dblk, shard, :], sem_in)
                ltu = p.dma("sync", dtu_c[it % 2][:, 512 * nn:512 * (nn + 1)],
                            dtua_o[nn, :, dblk, shard, :], sem_in)
            lt = ltu
            scan_tok = None
            for n in range(NST):
                col = NST * dblk + n
                gi = it * NST + n
                if n == 0:
                    p.wait("scalar", lt)
                if it == 0 and n == 0:
                    p.wait("scalar", a_tok)
                need = gi - 5
                if need >= 0:
                    c = need + ((3 - need) % 4)
                    p.wait("scalar", scan_toks.get(c))
                e_tok = p.op("scalar", "activation", sem="sS",
                             out=dA[gi % 5][:], in_=dt_c[it % 2][:],
                             func=AF.Exp, bias=0.0, scale=a_sb[:, col:col + 1])
                p.wait("vector", e_tok)
                if n == 0:
                    p.wait("vector", ltu)
                    sall_pp.guard("vector", it)
                init = 0.0 if tc == 0 else \
                    s_n[n][(it - 1) % 2][:, TC - 1:TC]
                sem = "sV" if (gi % 4 == 3) else None
                scan_tok = p.op("vector", "tensor_tensor_scan", sem=sem,
                                out=s_n[n][it % 2][:],
                                data0=dA[gi % 5][:], data1=dtu_c[it % 2][:],
                                initial=init, op0=ALU.mult, op1=ALU.add)
                if scan_tok is not None:
                    scan_toks[gi] = scan_tok
            scan_tok = scan_toks[it * NST + NST - 1]
            in_pp.done(it, scan_tok)
            grp_last = None
            for j in range(TC // 512):
                p.wait("tensor", scan_tok)
                pY_pp.guard("tensor", 4 * it + j)
                if it == 0 and j < 2:
                    p.wait("tensor", io.get("psum_prev", [None, None])[j])
                if it == 0 and j == 0:
                    p.wait("tensor", dg_tok)
                tok = None
                for n in range(NST):
                    col = NST * dblk + n
                    tok = p.op("tensor", "matmul",
                               sem="sT" if n == NST - 1 else None,
                               out=pY[j % 2][:, 0:512],
                               lhsT=diagG[:, 128 * col:128 * (col + 1)],
                               rhs=s_n[n][it % 2][:, 512 * j:512 * (j + 1)],
                               start=(n == 0), stop=(n == NST - 1))
                grp_last = tok
                p.wait("scalar", tok)
                ysb_pp.guard("scalar", 4 * it + j)
                ev = p.op("scalar", "copy", sem="sS",
                          out=ysb[j % 2][:], in_=pY[j % 2][:, 0:512])
                pY_pp.done(4 * it + j, ev)
                p.wait("gpsimd", ev)
                st = p.dma("gpsimd",
                           ya_in[j, 4 * b + tc, r0:r0 + 128, :],
                           ysb[j % 2][:], "sY0" if j % 2 == 0 else "sY1")
                ysb_pp.done(4 * it + j, st)
            sall_pp.done(it, grp_last)
            it += 1

    p.phase_end()
    return {
        "ystores": [p.total("sY0"), p.total("sY1")],
        "psum_free": [pY_pp.t[0], pY_pp.t[1]],
    }


def build_P2():
    nc = bass.Bass("TRN2", target_bir_lowering=False)
    p = Prog(nc)
    io = {
        "dta_o": nc.dram_tensor("dt_sl", [4, 128, 2, 8, 512], F32, kind="ExternalInput").ap(),
        "dtua_o": nc.dram_tensor("dtu_sl", [4, 128, 2, 8, 512], F32, kind="ExternalInput").ap(),
        "alog": nc.dram_tensor("alog", [128, 2 * NST], F32, kind="ExternalInput").ap(),
        "bp": nc.dram_tensor("bp", [128, 2 * NST], F32, kind="ExternalInput").ap(),
        "cp": nc.dram_tensor("cp", [128, 2 * NST], F32, kind="ExternalInput").ap(),
        "idf": nc.dram_tensor("idf", [128, 128], F32, kind="ExternalInput").ap(),
        "ya_in": nc.dram_tensor("y_sl", [4, 8, DSL, 512], F32, kind="ExternalOutput").ap(),
        "pY": [nc.psum_tensor(f"pY{i}", [128, 512], F32).__enter__() for i in range(2)],
    }
    r = emit_P2(p, io)
    for t in r["ystores"]:
        p.wait("gpsimd", t)
    p.build_block()
    return nc



def emit_P3(p, io):
    (x_tok, ya_o, u_in, g_in, W_outT, W1T, W2T, b1b, b2b, d_p, idb, cst, idf,
     onesr, out_t, x2_sp, xq2_sp, pA, pT_, pR) = (
        io["x_tok"], io["ya_o"], io["u_in"], io["g_in"], io["W_outT"],
        io["W1T"], io["W2T"], io["b1b"], io["b2b"], io["d_p"], io["idb"],
        io["cst"], io["idf"], io["onesr"], io["out_t"], io["x2_sp"],
        io["xq2_sp"], io["pA"], io["pT_"], io["pR"])
    NT = TLOC // 128
    p.phase_begin()

    segs = [p.sbuf(f"seg{m}", [128, 1024], BF16) for m in range(32)]
    woxs = [p.sbuf(f"wox{k}", [128, 1024], BF16) for k in range(16)]
    xqf = [p.sbuf(f"xqf{j}", [128, TLOC], BF16) for j in range(8)]
    b2b_sb = p.sbuf("b2b_sb", [128, DM], F32)
    d_sb = p.sbuf("d_sb", [128, 16], F32)
    idb_sb = p.sbuf("idb_sb", [128, 128], BF16)
    qg = p.sbuf("qg", [128, 1], F32)
    xs_all = p.sbuf("xs_all", [128, NT], F32)
    s2_all = p.sbuf("s2_all", [128, NT], F32)
    w1s = p.sbuf("w1s", [128, 1], F32)
    w2s = p.sbuf("w2s", [128, 1], F32)
    wsum = p.sbuf("wsum", [128, 1], F32)
    wtmp = [p.sbuf(f"wtmp{i}", [128, 1], F32) for i in range(2)]
    g_wrec = [p.sbuf(f"wrec{i}", [128, 1], F32) for i in range(2)]
    w2rec = [p.sbuf(f"w2rec{i}", [128, 1], F32) for i in range(2)]
    st6 = p.sbuf("st6", [128, 12], F32)
    mv = p.sbuf("mv", [128, 2], F32)
    rln = p.sbuf("rln", [128, 1], F32)
    xs_t = p.sbuf("xs_t", [128, 1], F32)
    hs_t = p.sbuf("hs_t", [128, 1], F32)
    s1_t = p.sbuf("s1_t", [128, 1], F32)
    xq2f = [p.sbuf(f"xq2f{i}", [128, 128], BF16) for i in range(2)]
    cst_sb = p.sbuf("cst_sb", [128, 4], F32)
    idf_sb = p.sbuf("idf_sb", [128, 128], F32)
    onesr_sb = p.sbuf("onesr_sb", [1, 128], F32)
    wrow = p.sbuf("wrow", [1, 128], F32)
    wsc = p.sbuf("wsc", [1, 4], F32)
    lnsum = p.sbuf("lnsum", [128, 4], F32)
    wstg = [p.sbuf(f"wstg{i}", [128, 1024], F32) for i in range(2)]
    wqmG = p.sbuf("wqmG", [128, 1024], F32)
    wqmV = p.sbuf("wqmV", [128, 1024], F32)

    def y2w(g, hh):
        return segs[2 * g + hh][:, 0:1024]

    def y2mm(k, tt):
        return segs[2 * k + tt // 8][:, 128 * (tt % 8):128 * (tt % 8) + 128]

    def wout(k):
        return woxs[k][:, 0:1024]

    def w2q_ap(k, c0, c1):
        if k < 16:
            return woxs[k][:, c0:c1]
        return segs[k - 16][:, c0:c1]

    for s in ("sCst", "sT", "sV", "sS", "sX",
              "sC0a", "sC0b", "sC0c", "sWo0", "sWo1", "sXt0", "sXt1",
              "sSg0", "sSg1", "sQl0", "sQl1", "sXr0", "sXr1",
              "sX2a", "sX2b", "sSp0", "sSp1", "sOt0", "sOt1", "sDmp",
              "sGw0", "sGw1", "sTw0", "sTw1", "sGq"):
        p.new_sem(s)

    tok_d = p.dma("sync", d_sb[:], d_p, "sCst")
    tok_id = p.dma("sync", idb_sb[:], idb, "sCst")
    tok_b = p.dma("sync", b2b_sb[:], b2b, "sCst")
    tok_cst = p.dma("sync", cst_sb[:], cst, "sCst")
    tok_idf = p.dma("sync", idf_sb[:], idf, "sCst")
    tok_d = tok_id = tok_b = tok_cst = tok_idf
    tok_onr = p.dma("sync", onesr_sb[:], onesr, "sX")
    p.op("vector", "memset", ap=qg[:], constant=1.0)
    p.op("vector", "tensor_scalar", out=qg[:], in0=qg[:],
         scalar1=2.0 ** -30, scalar2=None, op0=ALU.add)
    qg_tok = p.op("vector", "tensor_scalar", sem="sV", out=qg[:], in0=qg[:],
                  scalar1=1.0, scalar2=2.0 ** 30, op0=ALU.subtract, op1=ALU.mult)

    p.phase_begin()
    wostg = [p.sbuf(f"wostg{i}", [128, DM], F32) for i in range(2)]
    HW = 1024
    ystr = [p.sbuf(f"ystr{i}", [128, HW], F32) for i in range(3)]
    ustr = [p.sbuf(f"ustr{i}", [128, HW], F32) for i in range(3)]
    gstr = [p.sbuf(f"gstr{i}", [128, HW], F32) for i in range(3)]
    sgt = [p.sbuf(f"sgt{i}", [128, HW], F32) for i in range(3)]
    y2_pp = PP(p, 3)
    sg_pp = PP(p, 3)
    y2_done = None
    for gi in range(32):
        g, hh = gi // 2, gi % 2
        rows = slice(128 * g, 128 * (g + 1))
        cols = slice(HW * hh, HW * (hh + 1))
        y2_pp.guard("sync", gi)
        sem_g = ("sC0a", "sC0b", "sC0c")[gi % 3]
        if gi == 0:
            if io.get("entry") is not None:
                p.wait("sync", io["entry"])
            for t in io.get("ug_stores", []):
                p.wait("sync", t)
        for nn in (2 * hh, 2 * hh + 1):
            p.dma("sync", ystr[gi % 3][:, 512 * (nn - 2 * hh):
                                       512 * (nn - 2 * hh) + 512],
                  ya_o[nn, :, g % 2, g // 2, :], sem_g)
        p.dma("sync", ustr[gi % 3][:], u_in[rows, cols], sem_g)
        ldg = p.dma("sync", gstr[gi % 3][:], g_in[rows, cols], sem_g)
        ldy = ldu = ldg
        p.wait("scalar", ldg)
        sg_pp.guard("scalar", gi)
        sg_tok = p.op("scalar", "activation", sem="sS", out=sgt[gi % 3][:],
                      in_=gstr[gi % 3][:], func=AF.Silu, bias=0.0, scale=1.0)
        p.wait("vector", ldu)
        p.wait("vector", ldy)
        if gi == 0:
            p.wait("vector", tok_d)
        p.op("vector", "scalar_tensor_tensor", out=ustr[gi % 3][:],
             in0=ustr[gi % 3][:], scalar=d_sb[:, g:g + 1], in1=ystr[gi % 3][:],
             op0=ALU.mult, op1=ALU.add)
        p.wait("vector", sg_tok)
        y2_done = p.op("vector", "tensor_tensor", sem="sV",
                       out=y2w(g, hh),
                       in0=ustr[gi % 3][:], in1=sgt[gi % 3][:], op=ALU.mult)
        y2_pp.done(gi, y2_done)
        sg_pp.done(gi, y2_done)
    p.phase_end()

    if DBG_STOP0:
        p.build_block()
        return nc
    wo_pp = PP(p, 2)
    wout_ready = None
    for k in range(16):
        wo_pp.guard("sync", k)
        t = p.dma("sync", wostg[k % 2][:], W_outT[128 * k:128 * (k + 1), :],
                  "sWo0" if k % 2 == 0 else "sWo1")
        p.wait("vector", t)
        wout_ready = p.op("vector", "tensor_copy", sem="sV", out=wout(k),
                          in_=wostg[k % 2][:])
        wo_pp.done(k, wout_ready)

    p.phase_begin()
    stgw = [p.sbuf(f"stgw{i}", [128, 1024], F32) for i in range(2)]
    stg_pp = PP(p, 2)
    stg_i = [0]
    wmean_state = {"rtok": None, "first": True}

    def wmean_step(wt_slice):
        si = stg_i[0]
        stg_pp.guard("sync", si)
        if si == 0:
            p.wait("sync", y2_done)
        lt = p.dma("sync", stgw[si % 2][:], wt_slice,
                   "sSg0" if si % 2 == 0 else "sSg1")
        p.wait("scalar", lt)
        s_tok = p.op("scalar", "activation", sem="sS", out=lnscr[:],
                     in_=stgw[si % 2][:], func=AF.Abs, bias=0.0, scale=1.0,
                     accum_out=wtmp[si % 2][:])
        p.wait("vector", s_tok)
        if wmean_state["first"]:
            if wmean_state.get("wsum_guard") is not None:
                p.wait("vector", wmean_state["wsum_guard"])
            rtok = p.op("vector", "tensor_copy", sem="sV", out=wsum[:],
                        in_=wtmp[si % 2][:])
            wmean_state["first"] = False
        else:
            rtok = p.op("vector", "tensor_tensor", sem="sV", out=wsum[:],
                        in0=wsum[:], in1=wtmp[si % 2][:], op=ALU.add)
        stg_pp.done(si, rtok)
        wmean_state["rtok"] = rtok
        stg_i[0] += 1


    w1sc_trt = [None]

    def w_scales(rtok, ws_out, wrec=None):
        p.wait("tensor", rtok)
        p.wait("tensor", tok_idf)
        trt = p.op("tensor", "transpose", sem="sT", out=pR[:, 0:128],
                   in_=wsum[:], identity=idf_sb[:])
        w1sc_trt[0] = trt
        p.wait("vector", trt)
        p.op("vector", "tensor_copy", out=wrow[:], in_=pR[:, 0:128])
        p.op("vector", "tensor_reduce", out=wsc[:, 0:1], in_=wrow[:], axis=AX,
             op=ALU.add)
        p.op("vector", "tensor_scalar", out=wsc[:, 1:2], in0=wsc[:, 0:1],
             scalar1=1.0 / (4096.0 * 1024.0), scalar2=EPS_Q, op0=ALU.mult,
             op1=ALU.max)
        p.op("vector", "reciprocal", out=wsc[:, 2:3], in_=wsc[:, 1:2])
        p.op("vector", "tensor_scalar", out=wsc[:, 3:4], in0=wsc[:, 2:3],
             scalar1=-1.0, scalar2=None, op0=ALU.mult)
        p.wait("vector", tok_onr)
        t1 = p.op("vector", "tensor_scalar", sem="sV", out=wrow[:],
                  in0=onesr_sb[:], scalar1=wsc[:, 1:2], scalar2=None,
                  op0=ALU.mult)
        p.wait("sync", t1)
        d1 = p.dma("sync", ws_out[:], wrow[:], "sX")
        p.wait("vector", d1)
        t2 = p.op("vector", "tensor_scalar", sem="sV", out=wrow[:],
                  in0=onesr_sb[:], scalar1=wsc[:, 2:3], scalar2=None,
                  op0=ALU.mult)
        p.wait("sync", t2)
        d2 = p.dma("sync", (wrec or g_wrec)[0][:], wrow[:], "sX")
        p.wait("vector", d2)
        t3 = p.op("vector", "tensor_scalar", sem="sV", out=wrow[:],
                  in0=onesr_sb[:], scalar1=wsc[:, 3:4], scalar2=None,
                  op0=ALU.mult)
        p.wait("sync", t3)
        d3 = p.dma("sync", (wrec or g_wrec)[1][:], wrow[:], "sX")
        p.wait("vector", d3)
        return d3

    def w_quant_half(WT_slice, wq_slice, wq_free_tok, wr):
        si = stg_i[0]
        stg_pp.guard("sync", si)
        lt = p.dma("sync", stgw[si % 2][:], WT_slice,
                   "sSg0" if si % 2 == 0 else "sSg1")
        p.wait("vector", lt)
        if wq_free_tok is not None:
            p.wait("vector", wq_free_tok)
        p.op("vector", "tensor_scalar", out=wq_slice, in0=stgw[si % 2][:],
             scalar1=wr[0][:], scalar2=0.5, op0=ALU.mult, op1=ALU.is_gt)
        p.op("vector", "tensor_scalar", out=wqmV[:], in0=stgw[si % 2][:],
             scalar1=wr[1][:], scalar2=0.5, op0=ALU.mult, op1=ALU.is_gt)
        qtok = p.op("vector", "tensor_tensor", sem="sV", out=wq_slice,
                    in0=wq_slice, in1=wqmV[:], op=ALU.subtract)
        stg_pp.done(si, qtok)
        stg_i[0] += 1
        return qtok

    xtt = [p.sbuf(f"xtt{i}", [128, DM], F32) for i in range(2)]
    x1t = [p.sbuf(f"x1t{i}", [128, DM], F32) for i in range(2)]
    x2t = [p.sbuf(f"x2t{i}", [128, DM], F32) for i in range(2)]
    a32t = p.sbuf("a32t", [128, DM], F32)
    a52t = p.sbuf("a52t", [128, DM], F32)
    m32t = p.sbuf("m32t", [128, DM], F32)
    ddt = p.sbuf("ddt", [128, DM], F32)
    xq_t = p.sbuf("xq_t", [128, DM], F32)
    lnscr = p.sbuf("lnscr", [128, DM], F32)

    pA_free = {}
    pA_seq = 0
    pT_free = {}
    pT_seq = 0
    xtt_pp = PP(p, 2)
    x2t_pp = PP(p, 2)
    mv_prev = None
    xq_read = None
    aa_prev = None
    mmA_last = None
    for tt in range(NT):
        xtt_pp.guard("sync", tt)
        if tt == 0:
            p.wait("sync", y2_done)
            p.wait("sync", wout_ready)
        xld = p.dma("sync", xtt[tt % 2][:], x_tok[128 * tt:128 * (tt + 1), :],
                    "sXt0" if tt % 2 == 0 else "sXt1")
        x1_last = None
        for nch in range(2):
            p.wait("tensor", pA_free.get(pA_seq - 2))
            if pA_seq < 2:
                p.wait("tensor", io.get("psum_prev", [None, None])[pA_seq])
            if pA_seq == 0:
                p.wait("tensor", y2_done)
                p.wait("tensor", wout_ready)
            tok = None
            for k in range(16):
                tok = p.op("tensor", "matmul", sem="sT" if k == 15 else None,
                           out=pA[pA_seq % 2][:, 0:512],
                           lhsT=y2mm(k, tt),
                           rhs=wout(k)[:, 512 * nch:512 * (nch + 1)],
                           start=(k == 0), stop=(k == 15))
            mmA_last = tok
            p.wait("vector", tok)
            p.wait("vector", xld)
            x1_last = p.op("vector", "tensor_tensor", sem="sV",
                           out=x1t[tt % 2][:, 512 * nch:512 * (nch + 1)],
                           in0=pA[pA_seq % 2][:, 0:512],
                           in1=xtt[tt % 2][:, 512 * nch:512 * (nch + 1)],
                           op=ALU.add)
            pA_free[pA_seq] = x1_last
            pA_seq += 1
        xtt_pp.done(tt, x1_last)
        p.wait("scalar", x1_last)
        if mv_prev is not None:
            p.wait("scalar", mv_prev)
        p.op("scalar", "activation", out=lnscr[:], in_=x1t[tt % 2][:],
             func=AF.Identity, bias=0.0, scale=1.0, accum_out=lnsum[:, 0:1])
        sq_tok = p.op("scalar", "activation", sem="sS", out=lnscr[:],
                      in_=x1t[tt % 2][:], func=AF.Square, bias=0.0, scale=1.0,
                      accum_out=lnsum[:, 1:2])
        p.wait("vector", sq_tok)
        p.op("vector", "tensor_scalar", out=mv[:, 0:1], in0=lnsum[:, 0:1],
             scalar1=1.0 / DM, scalar2=None, op0=ALU.mult)
        p.op("vector", "scalar_tensor_tensor", out=lnsum[:, 2:3],
             in0=mv[:, 0:1], scalar=mv[:, 0:1], in1=lnsum[:, 1:2],
             op0=ALU.mult, op1=ALU.bypass)
        mv_tok = p.op("vector", "scalar_tensor_tensor", sem="sV",
                      out=mv[:, 1:2], in0=lnsum[:, 1:2], scalar=1.0 / DM,
                      in1=lnsum[:, 2:3], op0=ALU.mult, op1=ALU.subtract)
        mv_prev = mv_tok
        p.wait("scalar", mv_tok)
        p.op("scalar", "activation", out=rln[:], in_=mv[:, 1:2], func=AF.Ln,
             bias=cst_sb[:, 2:3], scale=1.0)
        r_tok = p.op("scalar", "activation", sem="sS", out=rln[:], in_=rln[:],
                     func=AF.Exp, bias=0.0, scale=-0.5)
        p.wait("vector", r_tok)
        x2t_pp.guard("vector", tt)
        x2_tok = p.op("vector", "tensor_scalar", sem="sV",
                      out=x2t[tt % 2][:], in0=x1t[tt % 2][:],
                      scalar1=mv[:, 0:1], scalar2=rln[:], op0=ALU.subtract,
                      op1=ALU.mult)
        p.wait("gpsimd", x2_tok)
        x2st = p.dma("gpsimd", x2_sp[128 * tt:128 * (tt + 1), :],
                     (x1t if DBG_X1 else x2t)[tt % 2][:],
                     "sX2a" if tt % 2 == 0 else "sX2b")
        p.op("vector", "tensor_reduce", out=xs_t[:], in_=x2t[tt % 2][:], axis=AX,
             op=ALU.max, apply_absolute_value=True)
        p.op("vector", "tensor_scalar", out=xs_t[:], in0=xs_t[:],
             scalar1=EPS_Q, scalar2=None, op0=ALU.max)
        p.op("vector", "tensor_copy", out=xs_all[:, tt:tt + 1], in_=xs_t[:])
        p.op("vector", "reciprocal", out=xs_t[:], in_=xs_t[:])
        qsc_tok = p.op("vector", "tensor_scalar", sem="sV", out=xs_t[:],
                       in0=xs_t[:], scalar1=QB, scalar2=None, op0=ALU.mult)
        if tt == 0:
            p.wait("vector", qg_tok)
        p.wait("scalar", qsc_tok)
        if aa_prev is not None:
            p.wait("scalar", aa_prev)
        p.op("scalar", "activation", out=a32t[:], in_=x2t[tt % 2][:],
             func=AF.Identity, bias=cst_sb[:, 0:1], scale=xs_t[:])
        p.op("scalar", "activation", out=a52t[:],
             in_=x2t[tt % 2][:], func=AF.Identity,
             bias=cst_sb[:, 1:2], scale=xs_t[:])
        aa_tok = p.op("scalar", "activation", sem="sS", out=m32t[:],
                      in_=a32t[:], func=AF.Identity, bias=cst_sb[:, 3:4],
                      scale=1.0)
        p.wait("vector", aa_tok)
        if xq_read is not None:
            p.wait("vector", xq_read)
        p.op("vector", "scalar_tensor_tensor", out=ddt[:], in0=a52t[:],
             scalar=C52, in1=m32t[:], op0=ALU.subtract, op1=ALU.subtract)
        if tt == 0:
            p.wait("vector", qg_tok)
        xq_tok = p.op("vector", "scalar_tensor_tensor", sem="sV",
                      out=xq_t[:], in0=ddt[:], scalar=qg[:],
                      in1=m32t[:], op0=ALU.mult, op1=ALU.add)
        aa_prev = xq_tok
        x2t_pp.done(tt, x2st)
        x2st_last = x2st
        tr_last = None
        for jq in range(2):
            for j4 in range(4):
                j = 4 * jq + j4
                p.wait("tensor", xq_tok)
                if j4 == 0:
                    p.wait("tensor", pT_free.get(pT_seq - 2))
                if pT_seq == 0 and j4 == 0:
                    p.wait("tensor", tok_idf)
                tr = p.op("tensor", "transpose",
                          sem="sT" if j4 == 3 else None,
                          out=pT_[pT_seq % 2][:, 128 * j4:128 * (j4 + 1)],
                          in_=xq_t[:, 128 * j:128 * (j + 1)],
                          identity=idf_sb[:])
            tr_last = tr
            p.wait("vector", tr)
            for j4 in range(4):
                j = 4 * jq + j4
                pT_free[pT_seq] = p.op(
                    "vector", "tensor_copy",
                    sem="sV" if j4 == 3 else None,
                    out=xqf[j][:, 128 * tt:128 * (tt + 1)],
                    in_=pT_[pT_seq % 2][:, 128 * j4:128 * (j4 + 1)])
            pT_seq += 1
        xq_read = tr_last
        for ii in range(4):
            i = 4 * tt + ii
            if i < 32:
                wmean_step(W1T[128 * (i // 4):128 * (i // 4) + 128,
                               1024 * (i % 4):1024 * (i % 4) + 1024])
            else:
                if i == 32:
                    w1sc = w_scales(wmean_state["rtok"], w1s)
                    wmean_state["first"] = True
                    wmean_state["wsum_guard"] = w1sc_trt[0]
                wmean_step(W2T[128 * (i - 32):128 * (i - 31), :])
    xqf_ready = pT_free[pT_seq - 1]
    w2sc = w_scales(wmean_state["rtok"], w2s, wrec=w2rec)

    w1q_tok = {}
    w2q_tok = {}
    w1_ready = None
    first_wq = mmA_last
    for h in range(4):
        for k in range(8):
            w1_ready = w_quant_half(
                W1T[128 * k:128 * (k + 1), 1024 * h:1024 * (h + 1)],
                segs[4 * k + h][:, 0:1024], first_wq, g_wrec)
            first_wq = None
            w1q_tok[(k, h)] = w1_ready
    p.phase_end()

    w2_lts = {}
    w2stg_free = {}

    def w2_load(i, eng):
        if i >= 32:
            return
        p.wait(eng, w2stg_free.get(i - 2))
        pre = "sGw" if eng == "gpsimd" else "sTw"
        w2_lts[i] = p.dma(eng, wstg[i % 2][:],
                          W2T[128 * i:128 * (i + 1), :],
                          pre + ("0" if i % 2 == 0 else "1"))

    def w2_quant(i, eng, scr, extra=()):
        p.wait(eng, w2_lts[i])
        for t in extra:
            p.wait(eng, t)
        dst = w2q_ap(i, 0, 1024)
        p.op(eng, "tensor_scalar", out=dst, in0=wstg[i % 2][:],
             scalar1=w2rec[0][:], scalar2=0.5, op0=ALU.mult, op1=ALU.is_gt)
        p.op(eng, "tensor_scalar", out=scr[:], in0=wstg[i % 2][:],
             scalar1=w2rec[1][:], scalar2=0.5, op0=ALU.mult, op1=ALU.is_gt)
        qt = p.op(eng, "tensor_tensor", sem="sGq" if eng == "gpsimd" else "sV",
                  out=dst, in0=dst, in1=scr[:], op=ALU.subtract)
        w2q_tok[i] = qt
        w2stg_free[i] = qt
        if (i + 2 < 16) == (i < 16):
            w2_load(i + 2, "gpsimd" if i < 16 else "sync")
        return qt

    p.phase_begin()
    b1b_sb = p.sbuf("b1b_sb", [128, 4 * DM], F32)
    p.wait("sync", w1_ready)
    p.wait("sync", p.total("sX2a"))
    p.wait("sync", p.total("sX2b"))
    p.wait("sync", xqf_ready)
    b1b_tok = p.dma("sync", b1b_sb[:], b1b, "sCst")
    w2_load(0, "gpsimd")
    w2_load(1, "gpsimd")
    for i_h in range(16):
        w2_quant(i_h, "gpsimd", wqmG,
                 extra=(mmA_last, w1sc, w2sc) if i_h == 0 else ())
    xq2g = [p.sbuf("xq2g0", [128, 4 * DM], BF16)] * 2
    h_t = p.sbuf("h_t", [128, 4 * DM], F32)
    a32h = p.sbuf("a32h", [128, 1024], F32)
    a52h = p.sbuf("a52h", [128, 1024], F32)
    xq2q = [p.sbuf(f"xq2q{i}", [128, 1024], F32) for i in range(2)]

    xq2f_pp = PP(p, 1)
    xq2q_free = {}
    f15_tok = {}
    gelu_last = None
    ah_prev = None
    xq2_read = None
    mmF_last = None
    sp_last = None
    hfree_prev = None
    for tt in range(NT):
        s1_tok = p.op("vector", "scalar_tensor_tensor", sem="sV", out=s1_t[:],
                      in0=xs_all[:, tt:tt + 1], scalar=INV_QB, in1=w1s[:],
                      op0=ALU.mult, op1=ALU.mult)
        for hch in range(8):
            p.wait("tensor", pA_free.get(pA_seq - 2))
            if pA_seq == 32:
                p.wait("tensor", xqf_ready)
            tok = None
            for k in range(8):
                if tt == 0:
                    p.wait("tensor", w1q_tok[(k, hch // 2)])
                tok = p.op("tensor", "matmul", sem="sT" if k == 7 else None,
                           out=pA[pA_seq % 2][:, 0:512],
                           lhsT=xqf[k][:, 128 * tt:128 * (tt + 1)],
                           rhs=segs[4 * k + hch // 2][:, 512 * (hch % 2):
                                    512 * (hch % 2) + 512],
                           start=(k == 0), stop=(k == 7))
            mmF_last = tok
            if tt == NT - 1:
                f15_tok[hch] = tok
            p.wait("vector", tok)
            p.wait("vector", s1_tok)
            if pA_seq == 32:
                p.wait("vector", b1b_tok)
            if hch == 0 and hfree_prev is not None:
                p.wait("vector", hfree_prev)
            z_tok = p.op("vector", "scalar_tensor_tensor", sem="sV",
                         out=h_t[:, 512 * hch:512 * (hch + 1)],
                         in0=pA[pA_seq % 2][:, 0:512],
                         scalar=s1_t[:],
                         in1=b1b_sb[:, 512 * hch:512 * (hch + 1)],
                         op0=ALU.mult, op1=ALU.add)
            pA_free[pA_seq] = z_tok
            p.wait("scalar", z_tok)
            gelu_last = p.op("scalar", "activation", sem="sS",
                             out=h_t[:, 512 * hch:512 * (hch + 1)],
                             in_=h_t[:, 512 * hch:512 * (hch + 1)],
                             func=AF.Gelu, bias=0.0, scale=1.0)
            pA_seq += 1
        p.wait("vector", gelu_last)
        p.op("vector", "tensor_reduce", out=hs_t[:], in_=h_t[:], axis=AX,
             op=ALU.max, apply_absolute_value=True)
        p.op("vector", "tensor_scalar", out=hs_t[:], in0=hs_t[:],
             scalar1=EPS_Q, scalar2=None, op0=ALU.max)
        if tt == 0:
            p.wait("vector", w2sc)
        p.op("vector", "scalar_tensor_tensor", out=s2_all[:, tt:tt + 1],
             in0=hs_t[:], scalar=INV_QB, in1=w2s[:], op0=ALU.mult, op1=ALU.mult)
        p.op("vector", "reciprocal", out=hs_t[:], in_=hs_t[:])
        q2_tok = p.op("vector", "tensor_scalar", sem="sV", out=hs_t[:],
                      in0=hs_t[:], scalar1=QB, scalar2=None, op0=ALU.mult)
        xq2_tok = None
        hfree = None
        tr_last = None
        cp_tok = None
        xq2f_pp.guard("vector", tt)
        for qh in range(4):
            sl = slice(1024 * qh, 1024 * (qh + 1))
            p.wait("scalar", q2_tok)
            if ah_prev is not None:
                p.wait("scalar", ah_prev)
            p.op("scalar", "activation", out=a32h[:], in_=h_t[:, sl],
                 func=AF.Identity, bias=cst_sb[:, 0:1], scale=hs_t[:])
            p.op("scalar", "activation", out=a52h[:],
                 in_=h_t[:, sl], func=AF.Identity, bias=cst_sb[:, 1:2],
                 scale=hs_t[:])
            ah_tok = p.op("scalar", "activation", sem="sS", out=a32h[:],
                          in_=a32h[:], func=AF.Identity, bias=cst_sb[:, 3:4],
                          scale=1.0)
            p.wait("vector", ah_tok)
            p.wait("vector", xq2q_free.get((tt, qh)))
            p.op("vector", "scalar_tensor_tensor", out=a52h[:], in0=a52h[:],
                 scalar=C52, in1=a32h[:], op0=ALU.subtract, op1=ALU.subtract)
            xq2_tok = p.op("vector", "scalar_tensor_tensor", sem="sV",
                           out=xq2q[qh % 2][:], in0=a52h[:], scalar=qg[:],
                           in1=a32h[:], op0=ALU.mult, op1=ALU.add)
            ah_prev = xq2_tok
            hfree = xq2_tok
            for jq2 in range(2):
                for j4 in range(4):
                    j = 4 * jq2 + j4
                    p.wait("tensor", xq2_tok)
                    if j4 == 0:
                        p.wait("tensor", pT_free.get(pT_seq - 2))
                    tr = p.op("tensor", "transpose",
                              sem="sT" if j4 == 3 else None,
                              out=pT_[pT_seq % 2][:, 128 * j4:128 * (j4 + 1)],
                              in_=xq2q[qh % 2][:, 128 * j:128 * (j + 1)],
                              identity=idf_sb[:])
                tr_last = tr
                p.wait("vector", tr)
                cp_tok = p.op("vector", "tensor_copy", sem="sV",
                              out=xq2g[tt % 2][:, 1024 * qh + 512 * jq2:
                                               1024 * qh + 512 * (jq2 + 1)],
                              in_=pT_[pT_seq % 2][:, 0:512])
                pT_free[pT_seq] = cp_tok
                pT_seq += 1
            nq = qh + 2
            xq2q_free[(tt, nq) if nq < 4 else (tt + 1, nq - 4)] = tr_last
        hfree_prev = hfree
        p.wait("sync", cp_tok)
        sp_last = p.dma("sync", xq2_sp[128 * tt:128 * (tt + 1), :],
                        xq2g[tt % 2][:], "sSp0" if tt % 2 == 0 else "sSp1")
        xq2f_pp.done(tt, sp_last)
        xq2_read = tr_last

    w2_load(16, "sync")
    w2_load(17, "sync")
    for i in range(16, 32):
        m = i - 16
        w2_quant(i, "vector", wqmV, extra=(f15_tok[2 * (m % 4) + 1],))

    p.phase_end()

    p.phase_begin()
    x2r = [p.sbuf(f"x2r{i}", [128, DM], F32) for i in range(2)]
    t2t = [p.sbuf(f"t2t{i}", [128, 512], F32) for i in range(2)]
    x3p = [p.sbuf(f"x3p{i}", [128, DM], F32) for i in range(2)]
    x3t = [p.sbuf(f"x3t{i}", [128, DM], F32) for i in range(2)]
    lnscr2 = p.sbuf("lnscr2", [128, DM], F32)
    xq2lt = [p.sbuf(f"xq2lt{i}", [128, 4 * DM], BF16) for i in range(2)]
    xq2l_pp = PP(p, 2)
    x2r_pp = PP(p, 2)
    t2_pp = PP(p, 2)
    x3t_pp = PP(p, 2)
    mv_prev2 = None
    for tt in range(NT):
        xq2l_pp.guard("sync", tt)
        if tt == 0:
            p.wait("sync", p.total("sSp0"))
            p.wait("sync", p.total("sSp1"))
            p.wait("sync", p.total("sX2a"))
            p.wait("sync", p.total("sX2b"))
            p.wait("sync", xq2_read)
        lq = p.dma("sync", xq2lt[tt % 2][:],
                   xq2_sp[128 * tt:128 * (tt + 1), :],
                   "sQl0" if tt % 2 == 0 else "sQl1")
        x2r_pp.guard("sync", tt)
        lx = p.dma("sync", x2r[tt % 2][:], x2_sp[128 * tt:128 * (tt + 1), :],
                   "sXr0" if tt % 2 == 0 else "sXr1")
        ev_last = None
        mm_last = None
        for nch in range(2):
            p.wait("tensor", pA_free.get(pA_seq - 2))
            p.wait("tensor", lq)
            tok = None
            for k in range(32):
                if tt == 0:
                    p.wait("tensor", w2q_tok[k])
                tok = p.op("tensor", "matmul", sem="sT" if k == 31 else None,
                           out=pA[pA_seq % 2][:, 0:512],
                           lhsT=xq2lt[tt % 2][:, 128 * k:128 * (k + 1)],
                           rhs=w2q_ap(k, 512 * nch, 512 * (nch + 1)),
                           start=(k == 0), stop=(k == 31))
            mm_last = tok
            p.wait("vector", tok)
            p.wait("vector", lx)
            if tt == 0 and nch == 0:
                p.wait("vector", tok_b)
            t2_pp.guard("vector", pA_seq)
            p.op("vector", "scalar_tensor_tensor",
                 out=t2t[pA_seq % 2][:], in0=pA[pA_seq % 2][:, 0:512],
                 scalar=s2_all[:, tt:tt + 1],
                 in1=b2b_sb[:, 512 * nch:512 * (nch + 1)],
                 op0=ALU.mult, op1=ALU.add)
            ev_last = p.op("vector", "tensor_tensor", sem="sV",
                           out=x3p[tt % 2][:, 512 * nch:512 * (nch + 1)],
                           in0=t2t[pA_seq % 2][:],
                           in1=x2r[tt % 2][:, 512 * nch:512 * (nch + 1)],
                           op=ALU.add)
            pA_free[pA_seq] = ev_last
            t2_pp.done(pA_seq, ev_last)
            pA_seq += 1
        xq2l_pp.done(tt, mm_last)
        x2r_pp.done(tt, ev_last)
        p.wait("scalar", ev_last)
        if mv_prev2 is not None:
            p.wait("scalar", mv_prev2)
        p.op("scalar", "activation", out=lnscr2[:], in_=x3p[tt % 2][:],
             func=AF.Identity, bias=0.0, scale=1.0, accum_out=lnsum[:, 0:1])
        sq_tok = p.op("scalar", "activation", sem="sS", out=lnscr2[:],
                      in_=x3p[tt % 2][:], func=AF.Square, bias=0.0, scale=1.0,
                      accum_out=lnsum[:, 1:2])
        p.wait("vector", sq_tok)
        p.op("vector", "tensor_scalar", out=mv[:, 0:1], in0=lnsum[:, 0:1],
             scalar1=1.0 / DM, scalar2=None, op0=ALU.mult)
        p.op("vector", "scalar_tensor_tensor", out=lnsum[:, 2:3],
             in0=mv[:, 0:1], scalar=mv[:, 0:1], in1=lnsum[:, 1:2],
             op0=ALU.mult, op1=ALU.bypass)
        mv_tok = p.op("vector", "scalar_tensor_tensor", sem="sV",
                      out=mv[:, 1:2], in0=lnsum[:, 1:2], scalar=1.0 / DM,
                      in1=lnsum[:, 2:3], op0=ALU.mult, op1=ALU.subtract)
        mv_prev2 = mv_tok
        p.wait("scalar", mv_tok)
        p.op("scalar", "activation", out=rln[:], in_=mv[:, 1:2], func=AF.Ln,
             bias=cst_sb[:, 2:3], scale=1.0)
        r_tok = p.op("scalar", "activation", sem="sS", out=rln[:], in_=rln[:],
                     func=AF.Exp, bias=0.0, scale=-0.5)
        p.wait("vector", r_tok)
        x3t_pp.guard("vector", tt)
        x3_tok = p.op("vector", "tensor_scalar", sem="sV",
                      out=x3t[tt % 2][:], in0=x3p[tt % 2][:],
                      scalar1=mv[:, 0:1], scalar2=rln[:], op0=ALU.subtract,
                      op1=ALU.mult)
        p.wait("gpsimd", x3_tok)
        ost = p.dma("gpsimd", out_t[128 * tt:128 * (tt + 1), :],
                    x3t[tt % 2][:], "sOt0" if tt % 2 == 0 else "sOt1")
        x3t_pp.done(tt, ost)
    p.phase_end()

    p.phase_end()
    return {"outs": [p.total(nm) for nm in
                     ("sOt0", "sOt1", "sX2a", "sX2b", "sSp0", "sSp1")]}


def build_P3():
    nc = bass.Bass("TRN2", target_bir_lowering=False)
    p = Prog(nc)
    io = {
        "x_tok": nc.dram_tensor("x_tok", [TLOC, DM], F32, kind="ExternalInput").ap(),
        "ya_o": nc.dram_tensor("y_tok", [4, 128, 2, 8, 512], F32, kind="ExternalInput").ap(),
        "u_in": nc.dram_tensor("u_in", [DI, TLOC], F32, kind="ExternalInput").ap(),
        "g_in": nc.dram_tensor("g_in", [DI, TLOC], F32, kind="ExternalInput").ap(),
        "W_outT": nc.dram_tensor("W_outT", [DI, DM], F32, kind="ExternalInput").ap(),
        "W1T": nc.dram_tensor("W1T", [DM, 4 * DM], F32, kind="ExternalInput").ap(),
        "W2T": nc.dram_tensor("W2T", [4 * DM, DM], F32, kind="ExternalInput").ap(),
        "b1b": nc.dram_tensor("b1b", [128, 4 * DM], F32, kind="ExternalInput").ap(),
        "b2b": nc.dram_tensor("b2b", [128, DM], F32, kind="ExternalInput").ap(),
        "d_p": nc.dram_tensor("d_p", [128, 16], F32, kind="ExternalInput").ap(),
        "idb": nc.dram_tensor("idb", [128, 128], BF16, kind="ExternalInput").ap(),
        "cst": nc.dram_tensor("cst", [128, 4], F32, kind="ExternalInput").ap(),
        "idf": nc.dram_tensor("idf", [128, 128], F32, kind="ExternalInput").ap(),
        "onesr": nc.dram_tensor("onesr", [1, 128], F32, kind="ExternalInput").ap(),
        "out_t": nc.dram_tensor("out_t", [TLOC, DM], F32, kind="ExternalOutput").ap(),
        "x2_sp": nc.dram_tensor("x2_sp", [TLOC, DM], F32, kind="ExternalOutput").ap(),
        "xq2_sp": nc.dram_tensor("xq2_sp", [TLOC, 4 * DM], BF16, kind="ExternalOutput").ap(),
        "pA": [nc.psum_tensor(f"pA{i}", [128, 512], F32).__enter__() for i in range(2)],
        "pT_": [nc.psum_tensor(f"pT{i}", [128, 512], F32).__enter__() for i in range(2)],
        "pR": nc.psum_tensor("pR", [1, 512], F32).__enter__(),
    }
    r = emit_P3(p, io)
    for t in r["outs"]:
        p.wait("gpsimd", t)
    p.build_block()
    return nc





def build_fused():
    nc = bass.Bass("TRN2", target_bir_lowering=False)
    p = Prog(nc)
    EI, EO, IN = "ExternalInput", "ExternalOutput", "Internal"
    d = nc.dram_tensor
    io1 = {
        "xT": d("xT", [DM, TEXT], F32, kind=EI).ap(),
        "W_inT": d("W_inT", [DM, 2 * DI], F32, kind=EI).ap(),
        "W_dtT": d("W_dtT", [DI, DI], F32, kind=EI).ap(),
        "cw": d("cw", [128, 16 * KCONV], F32, kind=EI).ap(),
        "cb": d("cb", [128, 16], F32, kind=EI).ap(),
        "bdt": d("bdt", [128, 16], F32, kind=EI).ap(),
        "dtb": d("dtb", [128, 16], F32, kind=EI).ap(),
        "u_out": d("u_sp", [DI, TLOC], F32, kind=IN).ap(),
        "g_out": d("g_sp", [DI, TLOC], F32, kind=IN).ap(),
        "dt_out": d("dta_i", [4, 8, DSL, 512], F32, kind=IN).ap(),
        "dtu_out": d("dtua_i", [4, 8, DSL, 512], F32, kind=IN).ap(),
    }
    io2 = {
        "dta_o": d("dta_o", [4, 128, 2, 8, 512], F32, kind=IN).ap(),
        "dtua_o": d("dtua_o", [4, 128, 2, 8, 512], F32, kind=IN).ap(),
        "alog": d("alog", [128, 2 * NST], F32, kind=EI).ap(),
        "bp": d("bp", [128, 2 * NST], F32, kind=EI).ap(),
        "cp": d("cp", [128, 2 * NST], F32, kind=EI).ap(),
        "ya_in": d("ya_i", [4, 8, DSL, 512], F32, kind=IN).ap(),
    }
    io3 = {
        "x_tok": d("x_tok", [TLOC, DM], F32, kind=EI).ap(),
        "ya_o": d("ya_o", [4, 128, 2, 8, 512], F32, kind=IN).ap(),
        "u_in": io1["u_out"], "g_in": io1["g_out"],
        "W_outT": d("W_outT", [DI, DM], F32, kind=EI).ap(),
        "W1T": d("W1T", [DM, 4 * DM], F32, kind=EI).ap(),
        "W2T": d("W2T", [4 * DM, DM], F32, kind=EI).ap(),
        "b1b": d("b1b", [128, 4 * DM], F32, kind=EI).ap(),
        "b2b": d("b2b", [128, DM], F32, kind=EI).ap(),
        "d_p": d("d_p", [128, 16], F32, kind=EI).ap(),
        "idb": d("idb", [128, 128], BF16, kind=EI).ap(),
        "cst": d("cst", [128, 4], F32, kind=EI).ap(),
        "idf": d("idf", [128, 128], F32, kind=EI).ap(),
        "onesr": d("onesr", [1, 128], F32, kind=EI).ap(),
        "out_t": d("out_t", [TLOC, DM], F32, kind=EO).ap(),
        "x2_sp": d("x2_sp", [TLOC, DM], F32, kind=IN).ap(),
        "xq2_sp": d("xq2_sp", [TLOC, 4 * DM], BF16, kind=IN).ap(),
    }
    idf_shared = io3["idf"]
    io2["idf"] = idf_shared

    psA = [p.psum(f"psA{i}", [128, 512]) for i in range(2)]
    psB = [p.psum(f"psB{i}", [128, 512]) for i in range(2)]
    pR = p.psum("pR", [1, 512])
    io1["pm"] = psA
    io2["pY"] = psA
    io3["pA"] = psA
    io3["pT_"] = psB
    io3["pR"] = pR

    p.prefix = "A_"
    r1 = emit_P1(p, io1)

    p.prefix = "X_"
    rsem_dt = p.new_sem("sRdt")
    rsem_du = p.new_sem("sRdu")
    rsem_y = p.new_sem("sRy")
    lsem = p.new_sem("sLoc")
    NTRIG = 4
    TW = TLOC // NTRIG

    def a2a_trigger(src_ap, dst_ap, n, rsem):
        src = src_ap[n].rearrange("s (dh dl) t -> s dh dl t", dh=2)
        dst = dst_ap[n]
        p.q["gpsimd"].append(
            lambda e, src=src, dst=dst: e.remote_dma_broadcast_einops(
                dst, src, p.sem[rsem], local_sem=p.sem[lsem],
                rearrange="R S dh dl t -> S dl dh R t"))
        p.cnt[rsem] += 16

    for t in r1["dtdu"]:
        p.wait("gpsimd", t)
    for n in range(NTRIG):
        a2a_trigger(io1["dt_out"], io2["dta_o"], n, rsem_dt)
        a2a_trigger(io1["dtu_out"], io2["dtua_o"], n, rsem_du)
    rs1a = (rsem_dt, p.cnt[rsem_dt])
    rs1b = (rsem_du, p.cnt[rsem_du])

    p.prefix = "B_"
    io2["entry"] = rs1a
    io2["entry2"] = rs1b
    io2["psum_prev"] = r1["psum_free"]
    r2 = emit_P2(p, io2)

    p.prefix = "X_"
    for t in r2["ystores"]:
        p.wait("gpsimd", t)
    for n in range(NTRIG):
        a2a_trigger(io2["ya_in"], io3["ya_o"], n, rsem_y)
    rs2 = (rsem_y, p.cnt[rsem_y])

    p.prefix = "C_"
    io3["entry"] = rs2
    io3["ug_stores"] = r1["ug"]
    io3["psum_prev"] = r2["psum_free"]
    r3 = emit_P3(p, io3)
    for t in r3["outs"]:
        p.wait("gpsimd", t)
    p.build_block()
    return nc



def kernel(L0_attn_hidden_states, L0_attn_W_in, L0_attn_W_out, L0_attn_W_dt,
           L0_attn_b_dt, L0_attn_conv_w, L0_attn_conv_b, L0_attn_A_log,
           L0_attn_B_param, L0_attn_C_param, L0_attn_D_param, L0_attn_dt_bias,
           L0_ffn_W1, L0_ffn_b1, L0_ffn_W2, L0_ffn_b2):
    f32 = np.float32
    x = np.asarray(L0_attn_hidden_states, f32)
    xf = np.ascontiguousarray(x.reshape(BATCH * L, DM))

    def colmajor16(v):
        return np.ascontiguousarray(np.asarray(v, f32).reshape(16, 128).T)

    W_inT = np.ascontiguousarray(np.asarray(L0_attn_W_in, f32).T)
    W_dtT = np.ascontiguousarray(np.asarray(L0_attn_W_dt, f32).T)
    W_outT = np.ascontiguousarray(np.asarray(L0_attn_W_out, f32).T)
    W1T = np.ascontiguousarray(np.asarray(L0_ffn_W1, f32).T)
    W2T = np.ascontiguousarray(np.asarray(L0_ffn_W2, f32).T)
    cw = np.ascontiguousarray(
        np.asarray(L0_attn_conv_w, f32).reshape(16, 128, KCONV)
        .transpose(1, 0, 2).reshape(128, 16 * KCONV))
    cb = colmajor16(L0_attn_conv_b)
    bdt = colmajor16(L0_attn_b_dt)
    dtb = colmajor16(L0_attn_dt_bias)
    d_p = colmajor16(L0_attn_D_param)
    b1b = np.ascontiguousarray(
        np.broadcast_to(np.asarray(L0_ffn_b1, f32), (128, 4 * DM)))
    b2b = np.ascontiguousarray(
        np.broadcast_to(np.asarray(L0_ffn_b2, f32), (128, DM)))
    idf = np.eye(128, dtype=f32)
    idb = np.eye(128, dtype=BF)
    cst = np.ascontiguousarray(np.broadcast_to(
        np.array([C23, C52, EPS_LN, -C23], f32), (128, 4)))

    ncA = build_P1()
    in_maps = []
    for c in range(NCORES):
        t0 = c * TLOC
        if t0 % L == 0:
            halo = np.zeros((HALO, DM), f32)
        else:
            halo = xf[t0 - HALO:t0]
        x_ext = np.concatenate([halo, xf[t0:t0 + TLOC]], axis=0)
        in_maps.append({
            "xT": np.ascontiguousarray(x_ext.T),
            "W_inT": W_inT, "W_dtT": W_dtT,
            "cw": cw, "cb": cb, "bdt": bdt, "dtb": dtb,
        })
    resA = run_bass_kernel_spmd(ncA, in_maps, core_ids=list(range(NCORES)))
    u_all = [np.asarray(r["u_out"]) for r in resA.results]
    g_all = [np.asarray(r["g_out"]) for r in resA.results]
    dt_all = [np.asarray(r["dt_out"]) for r in resA.results]
    dtu_all = [np.asarray(r["dtu_out"]) for r in resA.results]

    def to_slices(parts):
        out = [np.empty((4, 128, 2, NCORES, 512), f32) for _ in range(NCORES)]
        for r in range(NCORES):
            pr = parts[r]
            for me in range(NCORES):
                blk = pr[:, me].reshape(4, 2, 128, 512)
                out[me][:, :, :, r, :] = blk.transpose(0, 2, 1, 3)
        return out

    dt_sl = to_slices(dt_all)
    dtu_sl = to_slices(dtu_all)

    def slice32(v):
        v = np.asarray(v, f32)
        return [np.ascontiguousarray(
            v[DSL * c:DSL * (c + 1)].reshape(2, 128, NST)
            .transpose(1, 0, 2).reshape(128, 2 * NST)) for c in range(NCORES)]

    al_sl = slice32(L0_attn_A_log)
    b_sl = slice32(L0_attn_B_param)
    c_sl = slice32(L0_attn_C_param)

    ncB = build_P2()
    in_maps = [{
        "dt_sl": dt_sl[c],
        "dtu_sl": dtu_sl[c],
        "alog": al_sl[c], "bp": b_sl[c], "cp": c_sl[c], "idf": idf,
    } for c in range(NCORES)]
    resB = run_bass_kernel_spmd(ncB, in_maps, core_ids=list(range(NCORES)))
    y_all = [np.asarray(r["y_sl"]) for r in resB.results]
    y_tok = to_slices(y_all)

    ncC = build_P3()
    in_maps = []
    for c in range(NCORES):
        t0 = c * TLOC
        in_maps.append({
            "x_tok": np.ascontiguousarray(xf[t0:t0 + TLOC]),
            "y_tok": np.ascontiguousarray(y_tok[c]),
            "u_in": u_all[c], "g_in": g_all[c],
            "W_outT": W_outT, "W1T": W1T, "W2T": W2T,
            "b1b": b1b, "b2b": b2b, "d_p": d_p, "idb": idb, "cst": cst,
            "idf": idf, "onesr": np.ones((1, 128), f32),
        })
    resC = run_bass_kernel_spmd(ncC, in_maps, core_ids=list(range(NCORES)))
    out = np.empty((BATCH * L, DM), f32)
    for c in range(NCORES):
        out[c * TLOC:(c + 1) * TLOC] = np.asarray(resC.results[c]["out_t"])
    return out.reshape(BATCH, L, DM)


# version 20 (final)
# speedup vs baseline: 1.1072x; 1.1072x over previous
import contextlib
import os
os.environ.setdefault("TRN_ALLOW_ICI_ALL_TO_ALL", "1")
import numpy as np
import ml_dtypes
import concourse.bass as bass
import concourse.mybir as mybir
import bass_rust
from concourse.bass_utils import run_bass_kernel_spmd

F32 = mybir.dt.float32
BF16 = mybir.dt.bfloat16
AF = mybir.ActivationFunctionType
ALU = mybir.AluOpType
AX = mybir.AxisListType.X

NCORES = 8
BATCH, L, DM = 2, 8192, 1024
DI, NST, KCONV = 2048, 16, 4
TLOC = 2048
HALO = 128
TEXT = TLOC + HALO
DSL = DI // NCORES
C23 = 1.5 * 2.0 ** 23
C52 = 1.5 * 2.0 ** 52
EPS_LN = 1e-5
EPS_Q = 1e-5
QB = 127.0
INV_QB = 1.0 / 127.0
BF = ml_dtypes.bfloat16
DBG_X1 = False
DBG_Y2 = False
DBG_STOP = False
DBG_STOP0 = False


class Prog:
    ENGS = ("sync", "scalar", "vector", "tensor", "gpsimd")

    def __init__(self, nc):
        self.nc = nc
        self.q = {e: [] for e in self.ENGS}
        self.sem = {}
        self.cnt = {}
        self.stack = contextlib.ExitStack()
        self.phases = []
        self.prefix = ""
        self.dirty = {}

    def _n(self, name):
        return self.prefix + name

    def new_sem(self, name):
        n = self._n(name)
        self.sem[n] = self.stack.enter_context(self.nc.semaphore(n))
        self.cnt[n] = 0
        return n

    def total(self, name):
        n = self._n(name)
        return (n, self.cnt[n])

    def sbuf(self, name, shape, dt):
        st = self.phases[-1] if self.phases else self.stack
        return st.enter_context(
            self.nc.sbuf_tensor(self._n(name), list(shape), dt))

    def psum(self, name, shape, dt=F32):
        return self.stack.enter_context(self.nc.psum_tensor(name, list(shape), dt))

    def phase_begin(self):
        self.phases.append(contextlib.ExitStack())

    def phase_end(self):
        self.phases.pop().close()

    def wait(self, eng, tok):
        if tok is None:
            return
        sem, val = tok
        if val <= 0:
            return
        h = self.sem[sem]
        self.q[eng].append(lambda e, h=h, v=val: e.wait_ge(h, v))

    def dma(self, eng, out, in_, sem):
        sem = self._n(sem)
        h = self.sem[sem]

        def f(e, out=out, in_=in_, h=h):
            e.dma_start(out=out, in_=in_).then_inc(h, 16)

        self.q[eng].append(f)
        self.cnt[sem] += 16
        return (sem, self.cnt[sem])

    WRITE_KW = ("out", "accum_out", "ap")

    def op(self, eng, name, sem=None, **kw):
        sem = self._n(sem) if sem is not None else None
        h = self.sem[sem] if sem is not None else None
        late = eng in ("vector", "gpsimd", "scalar")
        pre_drain = False
        if late:
            dw, dr = self.dirty.setdefault(eng, (set(), set()))
            rd, wr = set(), set()
            for k, v in kw.items():
                if isinstance(v, bass_rust.AP):
                    (wr if k in self.WRITE_KW else rd).add(v.name)
            if ((rd | wr) & dw) or (wr & dr):
                pre_drain = True
                dw.clear()
                dr.clear()
            if h is not None:
                dw.clear()
                dr.clear()
            else:
                dw |= wr
                dr |= rd

        def f(e, name=name, kw=kw, h=h, late=late, pre=pre_drain):
            if pre:
                e.drain()
            i = getattr(e, name)(**kw)
            if late:
                if h is not None:
                    e.drain()
                    e.sem_inc(h, 1)
            elif h is not None:
                i.then_inc(h, 1)

        self.q[eng].append(f)
        if sem is not None:
            self.cnt[sem] += 1
            return (sem, self.cnt[sem])
        return None

    def build_block(self):
        nc = self.nc
        with nc.Block() as block:
            @block.sync
            def _(eng):
                for f in self.q["sync"]:
                    f(eng)

            @block.scalar
            def _(eng):
                for f in self.q["scalar"]:
                    f(eng)

            @block.vector
            def _(eng):
                for f in self.q["vector"]:
                    f(eng)

            @block.tensor
            def _(eng):
                for f in self.q["tensor"]:
                    f(eng)

            @block.gpsimd
            def _(eng):
                for f in self.q["gpsimd"]:
                    f(eng)
        self.stack.close()


class PP:
    def __init__(self, p, n=2):
        self.p, self.n, self.t = p, n, [None] * n

    def guard(self, eng, i):
        self.p.wait(eng, self.t[i % self.n])

    def done(self, i, tok):
        self.t[i % self.n] = tok



def emit_P1(p, io):
    (xT, W_inT, W_dtT, cw, cb, bdt, dtb, u_out, g_out, dt_out, dtu_out, pm) = (
        io["xT"], io["W_inT"], io["W_dtT"], io["cw"], io["cb"], io["bdt"],
        io["dtb"], io["u_out"], io["g_out"], io["dt_out"], io["dtu_out"],
        io["pm"])
    p.phase_begin()
    xbf = [p.sbuf(f"xbf{k}", [128, TEXT], BF16) for k in range(8)]
    wreg = p.sbuf("wreg", [128, 32768], BF16)
    stg = [p.sbuf(f"stg{i}", [128, TEXT], F32) for i in range(2)]
    xzf = [p.sbuf(f"xzf{i}", [128, TEXT], F32) for i in range(2)]
    cacc = [p.sbuf(f"cacc{i}", [128, TLOC], F32) for i in range(2)]
    ubfc = [p.sbuf(f"ubfc{i}", [128, 16 * 512], BF16) for i in range(2)]
    urel = [p.sbuf(f"urel{i}", [128, 512], F32) for i in range(2)]
    dt_t = [p.sbuf(f"dtt{i}", [128, 512], F32) for i in range(2)]
    dtu_t = [p.sbuf(f"dtut{i}", [128, 512], F32) for i in range(2)]
    cw_sb = p.sbuf("cw_sb", [128, 16 * KCONV], F32)
    cb_sb = p.sbuf("cb_sb", [128, 16], F32)
    bdt_sb = p.sbuf("bdt_sb", [128, 16], F32)
    dtb_sb = p.sbuf("dtb_sb", [128, 16], F32)
    spb_sb = p.sbuf("spb_sb", [128, 16], F32)

    for s in ("sCst", "sT", "sV", "sS", "sStg0", "sStg1", "sUr0", "sUr1",
              "sU0", "sU1", "sG0", "sG1", "sDt0", "sDt1", "sDu0", "sDu1"):
        p.new_sem(s)

    stg_pp = PP(p, 2)
    stg_n = [0]

    def staged_cast(dram_slice, width, cast_out):
        i = stg_n[0]
        stg_pp.guard("sync", i)
        t = p.dma("sync", stg[i % 2][:, 0:width], dram_slice,
                  "sStg0" if i % 2 == 0 else "sStg1")
        p.wait("vector", t)
        tok = p.op("vector", "tensor_copy", sem="sV",
                   out=cast_out, in_=stg[i % 2][:, 0:width])
        stg_pp.done(i, tok)
        stg_n[0] += 1
        return tok

    for k in range(8):
        staged_cast(xT[128 * k:128 * (k + 1), :], TEXT, xbf[k][:])
    win_tok = None
    for k in range(8):
        for hh in range(2):
            win_tok = staged_cast(
                W_inT[128 * k:128 * (k + 1), 2048 * hh:2048 * (hh + 1)], 2048,
                wreg[:, 4096 * k + 2048 * hh:4096 * k + 2048 * (hh + 1)])

    p.dma("sync", cw_sb[:], cw, "sCst")
    p.dma("sync", cb_sb[:], cb, "sCst")
    p.dma("sync", bdt_sb[:], bdt, "sCst")
    consts_tok = p.dma("sync", dtb_sb[:], dtb, "sCst")
    p.wait("vector", consts_tok)
    spb_tok = p.op("vector", "tensor_tensor", sem="sV",
                   out=spb_sb[:], in0=bdt_sb[:], in1=dtb_sb[:], op=ALU.add)

    CH_X = [(0, 512), (512, 512), (1024, 512), (1536, 512), (2048, 128)]
    CH_G = [(128, 512), (640, 512), (1152, 512), (1664, 512)]

    psum_free = {}
    gseq = 0
    gt_pp = PP(p, 2)
    cacc_pp = PP(p, 2)
    last_xz_mm = None
    gate_gi = 0

    for m in range(32):
        chunks = CH_X if m < 16 else CH_G
        for ci, (c0, cn) in enumerate(chunks):
            if gseq == 0:
                p.wait("tensor", win_tok)
            p.wait("tensor", psum_free.get(gseq - 2))
            tok = None
            for k in range(8):
                tok = p.op("tensor", "matmul", sem="sT" if k == 7 else None,
                           out=pm[gseq % 2][:, 0:cn],
                           lhsT=wreg[:, 4096 * k + 128 * m:4096 * k + 128 * m + 128],
                           rhs=xbf[k][:, c0:c0 + cn],
                           start=(k == 0), stop=(k == 7))
            last_xz_mm = tok
            p.wait("vector", tok)
            if m < 16:
                psum_free[gseq] = p.op("vector", "tensor_copy", sem="sV",
                                       out=xzf[m % 2][:, c0:c0 + cn],
                                       in_=pm[gseq % 2][:, 0:cn])
            else:
                gt_pp.guard("vector", gate_gi)
                ev = p.op("vector", "tensor_copy", sem="sV",
                          out=dtu_t[gate_gi % 2][:, 0:cn],
                          in_=pm[gseq % 2][:, 0:cn])
                psum_free[gseq] = ev
                p.wait("gpsimd", ev)
                st = p.dma("gpsimd",
                           g_out[128 * (m - 16):128 * (m - 15),
                                 c0 - HALO:c0 - HALO + cn],
                           dtu_t[gate_gi % 2][:, 0:cn],
                           "sG0" if gate_gi % 2 == 0 else "sG1")
                gt_pp.done(gate_gi, st)
                gate_gi += 1
            gseq += 1

        if m < 16:
            cacc_pp.guard("vector", m)
            if m == 0:
                p.wait("vector", consts_tok)
            p.op("vector", "tensor_scalar",
                 out=cacc[m % 2][:], in0=xzf[m % 2][:, 125:125 + TLOC],
                 scalar1=cw_sb[:, 4 * m:4 * m + 1], scalar2=None, op0=ALU.mult)
            conv_tok = None
            for j in (1, 2, 3):
                conv_tok = p.op("vector", "scalar_tensor_tensor",
                                sem="sV" if j == 3 else None,
                                out=cacc[m % 2][:],
                                in0=xzf[m % 2][:, 125 + j:125 + j + TLOC],
                                scalar=cw_sb[:, 4 * m + j:4 * m + j + 1],
                                in1=cacc[m % 2][:], op0=ALU.mult, op1=ALU.add)
            p.wait("scalar", conv_tok)
            silu_tok = p.op("scalar", "activation", sem="sS",
                            out=cacc[m % 2][:], in_=cacc[m % 2][:],
                            func=AF.Silu, bias=cb_sb[:, m:m + 1], scale=1.0)
            p.wait("gpsimd", silu_tok)
            ust = p.dma("gpsimd", u_out[128 * m:128 * (m + 1), :],
                        cacc[m % 2][:], "sU0" if m % 2 == 0 else "sU1")
            cacc_pp.done(m, ust)

    u_done = [p.total("sU0"), p.total("sU1")]

    wdt_ready = None
    for k in range(16):
        i = stg_n[0]
        stg_pp.guard("sync", i)
        t = p.dma("sync", stg[i % 2][:, 0:2048],
                  W_dtT[128 * k:128 * (k + 1), :],
                  "sStg0" if i % 2 == 0 else "sStg1")
        p.wait("vector", t)
        if k == 0:
            p.wait("vector", last_xz_mm)
        wdt_ready = p.op("vector", "tensor_copy", sem="sV",
                         out=wreg[:, 2048 * k:2048 * (k + 1)],
                         in_=stg[i % 2][:, 0:2048])
        stg_pp.done(i, wdt_ready)
        stg_n[0] += 1

    dtut_pp0 = [gt_pp.t[0], gt_pp.t[1]]
    ubfc_pp = PP(p, 2)
    urel_pp = PP(p, 2)
    dtt_pp = PP(p, 2)
    dtut_pp = PP(p, 2)
    dtut_pp.t = dtut_pp0
    dtu_guard = {}
    di = 0
    for n in range(4):
        c0 = 512 * n
        if n == 0:
            p.wait("sync", u_done[0])
            p.wait("sync", u_done[1])
        ub_tok = None
        for k in range(16):
            i = stg_n[0]
            stg_pp.guard("sync", i)
            t = p.dma("sync", stg[i % 2][:, 0:512],
                      u_out[128 * k:128 * (k + 1), c0:c0 + 512],
                      "sStg0" if i % 2 == 0 else "sStg1")
            p.wait("vector", t)
            if k == 0:
                ubfc_pp.guard("vector", n)
            ub_tok = p.op("vector", "tensor_copy", sem="sV",
                          out=ubfc[n % 2][:, 512 * k:512 * (k + 1)],
                          in_=stg[i % 2][:, 0:512])
            stg_pp.done(i, ub_tok)
            stg_n[0] += 1
        for m in range(16):
            urel_pp.guard("sync", di)
            ur_tok = p.dma("sync", urel[di % 2][:],
                           u_out[128 * m:128 * (m + 1), c0:c0 + 512],
                           "sUr0" if di % 2 == 0 else "sUr1")
            if di == 0:
                p.wait("tensor", wdt_ready)
            if m == 0:
                p.wait("tensor", ub_tok)
            p.wait("tensor", psum_free.get(gseq - 2))
            tok = None
            for k in range(16):
                tok = p.op("tensor", "matmul", sem="sT" if k == 15 else None,
                           out=pm[gseq % 2][:, 0:512],
                           lhsT=wreg[:, 2048 * k + 128 * m:2048 * k + 128 * m + 128],
                           rhs=ubfc[n % 2][:, 512 * k:512 * (k + 1)],
                           start=(k == 0), stop=(k == 15))
            if m == 15:
                ubfc_pp.done(n, tok)
            p.wait("scalar", tok)
            if di == 0:
                p.wait("scalar", spb_tok)
            dtt_pp.guard("scalar", di)
            p.wait("scalar", dtu_guard.get(di - 2))
            exp_tok = p.op("scalar", "activation", sem="sS",
                           out=dt_t[di % 2][:], in_=pm[gseq % 2][:, 0:512],
                           func=AF.Exp, bias=spb_sb[:, m:m + 1], scale=1.0)
            psum_free[gseq] = exp_tok
            ln_tok = p.op("scalar", "activation", sem="sS",
                          out=dt_t[di % 2][:], in_=dt_t[di % 2][:],
                          func=AF.Ln, bias=1.0, scale=1.0)
            p.wait("vector", ln_tok)
            p.wait("vector", ur_tok)
            dtut_pp.guard("vector", di)
            dtu_tok = p.op("vector", "tensor_tensor", sem="sV",
                           out=dtu_t[di % 2][:], in0=dt_t[di % 2][:],
                           in1=urel[di % 2][:], op=ALU.mult)
            dtu_guard[di] = dtu_tok
            urel_pp.done(di, dtu_tok)
            p.wait("gpsimd", ln_tok)
            st1 = p.dma("gpsimd",
                        dt_out[n, m // 2, 128 * (m % 2):128 * (m % 2) + 128, :],
                        dt_t[di % 2][:], "sDt0" if di % 2 == 0 else "sDt1")
            p.wait("gpsimd", dtu_tok)
            st2 = p.dma("gpsimd",
                        dtu_out[n, m // 2, 128 * (m % 2):128 * (m % 2) + 128, :],
                        dtu_t[di % 2][:], "sDu0" if di % 2 == 0 else "sDu1")
            dtt_pp.done(di, st1)
            dtut_pp.done(di, st2)
            gseq += 1
            di += 1

    p.phase_end()
    return {
        "stores": [p.total(nm) for nm in
                   ("sU0", "sU1", "sG0", "sG1", "sDt0", "sDt1", "sDu0", "sDu1")],
        "dtdu": [p.total(nm) for nm in ("sDt0", "sDt1", "sDu0", "sDu1")],
        "ug": [p.total(nm) for nm in ("sU0", "sU1", "sG0", "sG1")],
        "psum_free": [psum_free.get(gseq - 2), psum_free.get(gseq - 1)],
    }


def build_P1():
    nc = bass.Bass("TRN2", target_bir_lowering=False)
    p = Prog(nc)
    io = {
        "xT": nc.dram_tensor("xT", [DM, TEXT], F32, kind="ExternalInput").ap(),
        "W_inT": nc.dram_tensor("W_inT", [DM, 2 * DI], F32, kind="ExternalInput").ap(),
        "W_dtT": nc.dram_tensor("W_dtT", [DI, DI], F32, kind="ExternalInput").ap(),
        "cw": nc.dram_tensor("cw", [128, 16 * KCONV], F32, kind="ExternalInput").ap(),
        "cb": nc.dram_tensor("cb", [128, 16], F32, kind="ExternalInput").ap(),
        "bdt": nc.dram_tensor("bdt", [128, 16], F32, kind="ExternalInput").ap(),
        "dtb": nc.dram_tensor("dtb", [128, 16], F32, kind="ExternalInput").ap(),
        "u_out": nc.dram_tensor("u_out", [DI, TLOC], F32, kind="ExternalOutput").ap(),
        "g_out": nc.dram_tensor("g_out", [DI, TLOC], F32, kind="ExternalOutput").ap(),
        "dt_out": nc.dram_tensor("dt_out", [4, 8, DSL, 512], F32, kind="ExternalOutput").ap(),
        "dtu_out": nc.dram_tensor("dtu_out", [4, 8, DSL, 512], F32, kind="ExternalOutput").ap(),
        "pm": [nc.psum_tensor(f"pm{i}", [128, 512], F32).__enter__() for i in range(2)],
    }
    r = emit_P1(p, io)
    for t in r["stores"]:
        p.wait("sync", t)
    p.build_block()
    return nc



def emit_P2(p, io):
    (dta_o, dtua_o, alog, bp, cp, idf, ya_in, pY) = (
        io["dta_o"], io["dtua_o"], io["alog"], io["bp"], io["cp"], io["idf"],
        io["ya_in"], io["pY"])
    TC = 2048
    NTC = L // TC
    p.phase_begin()
    al_sb = p.sbuf("al_sb", [128, 2 * NST], F32)
    a_sb = p.sbuf("a_sb", [128, 2 * NST], F32)
    ax_sb = p.sbuf("ax_sb", [128, 2 * NST], F32)
    b_sb = p.sbuf("b_sb", [128, 2 * NST], F32)
    c_sb = p.sbuf("c_sb", [128, 2 * NST], F32)
    g_sb = p.sbuf("g_sb", [128, 2 * NST], F32)
    id_sb = p.sbuf("id_sb", [128, 128], F32)
    diagG = p.sbuf("diagG", [128, 2 * NST * 128], BF16)
    dt_c = [p.sbuf(f"dtc{i}", [128, TC], F32) for i in range(2)]
    dtu_c = [p.sbuf(f"dtuc{i}", [128, TC], F32) for i in range(2)]
    dA = [p.sbuf(f"dA{i}", [128, TC], F32) for i in range(4)]
    dA.append(p.psum("dA4", [128, TC]))
    ysb = [p.sbuf(f"ysb{i}", [128, 512], F32) for i in range(2)]
    s_n = [[p.sbuf(f"s_{n}_{i}", [128, TC], BF16) for i in range(2)]
           for n in range(NST)]

    for s in ("sCst", "sT", "sV", "sS", "sIn0", "sIn1", "sY0", "sY1"):
        p.new_sem(s)

    p.dma("sync", al_sb[:], alog, "sCst")
    p.dma("sync", b_sb[:], bp, "sCst")
    p.dma("sync", c_sb[:], cp, "sCst")
    t3 = p.dma("sync", id_sb[:], idf, "sCst")
    t0 = t3
    t2 = t3
    p.wait("scalar", t0)
    tok = p.op("scalar", "activation", sem="sS", out=ax_sb[:], in_=al_sb[:],
               func=AF.Exp, bias=0.0, scale=1.0)
    p.wait("vector", tok)
    a_tok = p.op("vector", "tensor_scalar", sem="sV", out=a_sb[:], in0=ax_sb[:],
                 scalar1=-1.0, scalar2=None, op0=ALU.mult)
    p.wait("vector", t2)
    p.op("vector", "tensor_tensor", out=g_sb[:], in0=b_sb[:], in1=c_sb[:],
         op=ALU.mult)
    p.wait("vector", t3)
    dg_tok = None
    for col in range(2 * NST):
        dg_tok = p.op("vector", "tensor_scalar", sem="sV",
                      out=diagG[:, 128 * col:128 * (col + 1)], in0=id_sb[:],
                      scalar1=g_sb[:, col:col + 1], scalar2=None, op0=ALU.mult)

    in_pp = PP(p, 2)
    scan_toks = {}
    sall_pp = PP(p, 2)
    pY_pp = PP(p, 2)
    ysb_pp = PP(p, 2)

    it = 0
    for ss in range(4):
        b, dblk = ss // 2, ss % 2
        r0 = 128 * dblk
        for tc in range(NTC):
            c0 = TC * tc
            in_pp.guard("sync", it)
            if it == 0:
                if io.get("entry") is not None:
                    p.wait("sync", io["entry"])
                if io.get("entry2") is not None:
                    p.wait("sync", io["entry2"])
            shard = 4 * b + tc
            sem_in = "sIn0" if it % 2 == 0 else "sIn1"
            for nn in range(4):
                p.dma("sync", dt_c[it % 2][:, 512 * nn:512 * (nn + 1)],
                      dta_o[nn, :, dblk, shard, :], sem_in)
                ltu = p.dma("sync", dtu_c[it % 2][:, 512 * nn:512 * (nn + 1)],
                            dtua_o[nn, :, dblk, shard, :], sem_in)
            lt = ltu
            scan_tok = None
            for n in range(NST):
                col = NST * dblk + n
                gi = it * NST + n
                if n == 0:
                    p.wait("scalar", lt)
                if it == 0 and n == 0:
                    p.wait("scalar", a_tok)
                need = gi - 5
                if need >= 0:
                    c = need + ((3 - need) % 4)
                    p.wait("scalar", scan_toks.get(c))
                e_tok = p.op("scalar", "activation", sem="sS",
                             out=dA[gi % 5][:], in_=dt_c[it % 2][:],
                             func=AF.Exp, bias=0.0, scale=a_sb[:, col:col + 1])
                p.wait("vector", e_tok)
                if n == 0:
                    p.wait("vector", ltu)
                    sall_pp.guard("vector", it)
                init = 0.0 if tc == 0 else \
                    s_n[n][(it - 1) % 2][:, TC - 1:TC]
                sem = "sV" if (gi % 4 == 3) else None
                scan_tok = p.op("vector", "tensor_tensor_scan", sem=sem,
                                out=s_n[n][it % 2][:],
                                data0=dA[gi % 5][:], data1=dtu_c[it % 2][:],
                                initial=init, op0=ALU.mult, op1=ALU.add)
                if scan_tok is not None:
                    scan_toks[gi] = scan_tok
            scan_tok = scan_toks[it * NST + NST - 1]
            in_pp.done(it, scan_tok)
            grp_last = None
            for j in range(TC // 512):
                p.wait("tensor", scan_tok)
                pY_pp.guard("tensor", 4 * it + j)
                if it == 0 and j < 2:
                    p.wait("tensor", io.get("psum_prev", [None, None])[j])
                if it == 0 and j == 0:
                    p.wait("tensor", dg_tok)
                tok = None
                for n in range(NST):
                    col = NST * dblk + n
                    tok = p.op("tensor", "matmul",
                               sem="sT" if n == NST - 1 else None,
                               out=pY[j % 2][:, 0:512],
                               lhsT=diagG[:, 128 * col:128 * (col + 1)],
                               rhs=s_n[n][it % 2][:, 512 * j:512 * (j + 1)],
                               start=(n == 0), stop=(n == NST - 1))
                grp_last = tok
                p.wait("scalar", tok)
                ysb_pp.guard("scalar", 4 * it + j)
                ev = p.op("scalar", "copy", sem="sS",
                          out=ysb[j % 2][:], in_=pY[j % 2][:, 0:512])
                pY_pp.done(4 * it + j, ev)
                p.wait("gpsimd", ev)
                st = p.dma("gpsimd",
                           ya_in[j, 4 * b + tc, r0:r0 + 128, :],
                           ysb[j % 2][:], "sY0" if j % 2 == 0 else "sY1")
                ysb_pp.done(4 * it + j, st)
            sall_pp.done(it, grp_last)
            it += 1

    p.phase_end()
    return {
        "ystores": [p.total("sY0"), p.total("sY1")],
        "psum_free": [pY_pp.t[0], pY_pp.t[1]],
    }


def build_P2():
    nc = bass.Bass("TRN2", target_bir_lowering=False)
    p = Prog(nc)
    io = {
        "dta_o": nc.dram_tensor("dt_sl", [4, 128, 2, 8, 512], F32, kind="ExternalInput").ap(),
        "dtua_o": nc.dram_tensor("dtu_sl", [4, 128, 2, 8, 512], F32, kind="ExternalInput").ap(),
        "alog": nc.dram_tensor("alog", [128, 2 * NST], F32, kind="ExternalInput").ap(),
        "bp": nc.dram_tensor("bp", [128, 2 * NST], F32, kind="ExternalInput").ap(),
        "cp": nc.dram_tensor("cp", [128, 2 * NST], F32, kind="ExternalInput").ap(),
        "idf": nc.dram_tensor("idf", [128, 128], F32, kind="ExternalInput").ap(),
        "ya_in": nc.dram_tensor("y_sl", [4, 8, DSL, 512], F32, kind="ExternalOutput").ap(),
        "pY": [nc.psum_tensor(f"pY{i}", [128, 512], F32).__enter__() for i in range(2)],
    }
    r = emit_P2(p, io)
    for t in r["ystores"]:
        p.wait("gpsimd", t)
    p.build_block()
    return nc



def emit_P3(p, io):
    (x_tok, ya_o, u_in, g_in, W_outT, W1T, W2T, b1b, b2b, d_p, idb, cst, idf,
     onesr, out_t, x2_sp, xq2_sp, pA, pT_, pR) = (
        io["x_tok"], io["ya_o"], io["u_in"], io["g_in"], io["W_outT"],
        io["W1T"], io["W2T"], io["b1b"], io["b2b"], io["d_p"], io["idb"],
        io["cst"], io["idf"], io["onesr"], io["out_t"], io["x2_sp"],
        io["xq2_sp"], io["pA"], io["pT_"], io["pR"])
    NT = TLOC // 128
    p.phase_begin()

    segs = [p.sbuf(f"seg{m}", [128, 1024], BF16) for m in range(32)]
    woxs = [p.sbuf(f"wox{k}", [128, 1024], BF16) for k in range(16)]
    xqf = [p.sbuf(f"xqf{j}", [128, TLOC], BF16) for j in range(8)]
    b2b_sb = p.sbuf("b2b_sb", [128, DM], F32)
    d_sb = p.sbuf("d_sb", [128, 16], F32)
    idb_sb = p.sbuf("idb_sb", [128, 128], BF16)
    qg = p.sbuf("qg", [128, 1], F32)
    xs_all = p.sbuf("xs_all", [128, NT], F32)
    s2_all = p.sbuf("s2_all", [128, NT], F32)
    w1s = p.sbuf("w1s", [128, 1], F32)
    w2s = p.sbuf("w2s", [128, 1], F32)
    wsum = p.sbuf("wsum", [128, 1], F32)
    wtmp = [p.sbuf(f"wtmp{i}", [128, 1], F32) for i in range(2)]
    g_wrec = [p.sbuf(f"wrec{i}", [128, 1], F32) for i in range(2)]
    w2rec = [p.sbuf(f"w2rec{i}", [128, 1], F32) for i in range(2)]
    st6 = p.sbuf("st6", [128, 12], F32)
    mv = p.sbuf("mv", [128, 2], F32)
    rln = p.sbuf("rln", [128, 1], F32)
    xs_t = p.sbuf("xs_t", [128, 1], F32)
    hs_t = p.sbuf("hs_t", [128, 1], F32)
    s1_t = p.sbuf("s1_t", [128, 1], F32)
    xq2f = [p.sbuf(f"xq2f{i}", [128, 128], BF16) for i in range(2)]
    cst_sb = p.sbuf("cst_sb", [128, 4], F32)
    idf_sb = p.sbuf("idf_sb", [128, 128], F32)
    onesr_sb = p.sbuf("onesr_sb", [1, 128], F32)
    wrow = p.sbuf("wrow", [1, 128], F32)
    wsc = p.sbuf("wsc", [1, 4], F32)
    lnsum = p.sbuf("lnsum", [128, 4], F32)
    wstg = [p.sbuf(f"wstg{i}", [128, 1024], F32) for i in range(2)]
    wqmG = p.sbuf("wqmG", [128, 1024], F32)
    wqmV = p.sbuf("wqmV", [128, 1024], F32)

    def y2w(g, hh):
        return segs[2 * g + hh][:, 0:1024]

    def y2mm(k, tt):
        return segs[2 * k + tt // 8][:, 128 * (tt % 8):128 * (tt % 8) + 128]

    def wout(k):
        return woxs[k][:, 0:1024]

    def w2q_ap(k, c0, c1):
        if k < 16:
            return woxs[k][:, c0:c1]
        return segs[k - 16][:, c0:c1]

    for s in ("sCst", "sT", "sV", "sS", "sX",
              "sC0a", "sC0b", "sC0c", "sWo0", "sWo1", "sXt0", "sXt1",
              "sSg0", "sSg1", "sQl0", "sQl1", "sXr0", "sXr1",
              "sX2a", "sX2b", "sSp0", "sSp1", "sOt0", "sOt1", "sDmp",
              "sGw0", "sGw1", "sTw0", "sTw1", "sGq"):
        p.new_sem(s)

    tok_d = p.dma("sync", d_sb[:], d_p, "sCst")
    tok_id = p.dma("sync", idb_sb[:], idb, "sCst")
    tok_b = p.dma("sync", b2b_sb[:], b2b, "sCst")
    tok_cst = p.dma("sync", cst_sb[:], cst, "sCst")
    tok_idf = p.dma("sync", idf_sb[:], idf, "sCst")
    tok_d = tok_id = tok_b = tok_cst = tok_idf
    tok_onr = p.dma("sync", onesr_sb[:], onesr, "sX")
    p.op("vector", "memset", ap=qg[:], constant=1.0)
    p.op("vector", "tensor_scalar", out=qg[:], in0=qg[:],
         scalar1=2.0 ** -30, scalar2=None, op0=ALU.add)
    qg_tok = p.op("vector", "tensor_scalar", sem="sV", out=qg[:], in0=qg[:],
                  scalar1=1.0, scalar2=2.0 ** 30, op0=ALU.subtract, op1=ALU.mult)

    p.phase_begin()
    wostg = [p.sbuf(f"wostg{i}", [128, DM], F32) for i in range(2)]
    HW = 1024
    ystr = [p.sbuf(f"ystr{i}", [128, HW], F32) for i in range(3)]
    ustr = [p.sbuf(f"ustr{i}", [128, HW], F32) for i in range(3)]
    gstr = [p.sbuf(f"gstr{i}", [128, HW], F32) for i in range(3)]
    sgt = [p.sbuf(f"sgt{i}", [128, HW], F32) for i in range(3)]
    y2_pp = PP(p, 3)
    sg_pp = PP(p, 3)
    y2_done = None
    for gi in range(32):
        g, hh = gi // 2, gi % 2
        rows = slice(128 * g, 128 * (g + 1))
        cols = slice(HW * hh, HW * (hh + 1))
        y2_pp.guard("sync", gi)
        sem_g = ("sC0a", "sC0b", "sC0c")[gi % 3]
        if gi == 0:
            if io.get("entry") is not None:
                p.wait("sync", io["entry"])
            for t in io.get("ug_stores", []):
                p.wait("sync", t)
        for nn in (2 * hh, 2 * hh + 1):
            p.dma("sync", ystr[gi % 3][:, 512 * (nn - 2 * hh):
                                       512 * (nn - 2 * hh) + 512],
                  ya_o[nn, :, g % 2, g // 2, :], sem_g)
        p.dma("sync", ustr[gi % 3][:], u_in[rows, cols], sem_g)
        ldg = p.dma("sync", gstr[gi % 3][:], g_in[rows, cols], sem_g)
        ldy = ldu = ldg
        p.wait("scalar", ldg)
        sg_pp.guard("scalar", gi)
        sg_tok = p.op("scalar", "activation", sem="sS", out=sgt[gi % 3][:],
                      in_=gstr[gi % 3][:], func=AF.Silu, bias=0.0, scale=1.0)
        p.wait("vector", ldu)
        p.wait("vector", ldy)
        if gi == 0:
            p.wait("vector", tok_d)
        p.op("vector", "scalar_tensor_tensor", out=ustr[gi % 3][:],
             in0=ustr[gi % 3][:], scalar=d_sb[:, g:g + 1], in1=ystr[gi % 3][:],
             op0=ALU.mult, op1=ALU.add)
        p.wait("vector", sg_tok)
        y2_done = p.op("vector", "tensor_tensor", sem="sV",
                       out=y2w(g, hh),
                       in0=ustr[gi % 3][:], in1=sgt[gi % 3][:], op=ALU.mult)
        y2_pp.done(gi, y2_done)
        sg_pp.done(gi, y2_done)
    p.phase_end()

    if DBG_STOP0:
        p.build_block()
        return nc
    wo_pp = PP(p, 2)
    wout_ready = None
    for k in range(16):
        wo_pp.guard("sync", k)
        t = p.dma("sync", wostg[k % 2][:], W_outT[128 * k:128 * (k + 1), :],
                  "sWo0" if k % 2 == 0 else "sWo1")
        p.wait("vector", t)
        wout_ready = p.op("vector", "tensor_copy", sem="sV", out=wout(k),
                          in_=wostg[k % 2][:])
        wo_pp.done(k, wout_ready)

    p.phase_begin()
    stgw = [p.sbuf(f"stgw{i}", [128, 1024], F32) for i in range(2)]
    stg_pp = PP(p, 2)
    stg_i = [0]
    wmean_state = {"rtok": None, "first": True}

    def wmean_step(wt_slice):
        si = stg_i[0]
        stg_pp.guard("sync", si)
        if si == 0:
            p.wait("sync", y2_done)
        lt = p.dma("sync", stgw[si % 2][:], wt_slice,
                   "sSg0" if si % 2 == 0 else "sSg1")
        p.wait("scalar", lt)
        s_tok = p.op("scalar", "activation", sem="sS", out=lnscr[:],
                     in_=stgw[si % 2][:], func=AF.Abs, bias=0.0, scale=1.0,
                     accum_out=wtmp[si % 2][:])
        p.wait("vector", s_tok)
        if wmean_state["first"]:
            if wmean_state.get("wsum_guard") is not None:
                p.wait("vector", wmean_state["wsum_guard"])
            rtok = p.op("vector", "tensor_copy", sem="sV", out=wsum[:],
                        in_=wtmp[si % 2][:])
            wmean_state["first"] = False
        else:
            rtok = p.op("vector", "tensor_tensor", sem="sV", out=wsum[:],
                        in0=wsum[:], in1=wtmp[si % 2][:], op=ALU.add)
        stg_pp.done(si, rtok)
        wmean_state["rtok"] = rtok
        stg_i[0] += 1


    w1sc_trt = [None]

    def w_scales(rtok, ws_out, wrec=None):
        p.wait("tensor", rtok)
        p.wait("tensor", tok_idf)
        trt = p.op("tensor", "transpose", sem="sT", out=pR[:, 0:128],
                   in_=wsum[:], identity=idf_sb[:])
        w1sc_trt[0] = trt
        p.wait("vector", trt)
        p.op("vector", "tensor_copy", out=wrow[:], in_=pR[:, 0:128])
        p.op("vector", "tensor_reduce", out=wsc[:, 0:1], in_=wrow[:], axis=AX,
             op=ALU.add)
        p.op("vector", "tensor_scalar", out=wsc[:, 1:2], in0=wsc[:, 0:1],
             scalar1=1.0 / (4096.0 * 1024.0), scalar2=EPS_Q, op0=ALU.mult,
             op1=ALU.max)
        p.op("vector", "reciprocal", out=wsc[:, 2:3], in_=wsc[:, 1:2])
        p.op("vector", "tensor_scalar", out=wsc[:, 3:4], in0=wsc[:, 2:3],
             scalar1=-1.0, scalar2=None, op0=ALU.mult)
        p.wait("vector", tok_onr)
        t1 = p.op("vector", "tensor_scalar", sem="sV", out=wrow[:],
                  in0=onesr_sb[:], scalar1=wsc[:, 1:2], scalar2=None,
                  op0=ALU.mult)
        p.wait("sync", t1)
        d1 = p.dma("sync", ws_out[:], wrow[:], "sX")
        p.wait("vector", d1)
        t2 = p.op("vector", "tensor_scalar", sem="sV", out=wrow[:],
                  in0=onesr_sb[:], scalar1=wsc[:, 2:3], scalar2=None,
                  op0=ALU.mult)
        p.wait("sync", t2)
        d2 = p.dma("sync", (wrec or g_wrec)[0][:], wrow[:], "sX")
        p.wait("vector", d2)
        t3 = p.op("vector", "tensor_scalar", sem="sV", out=wrow[:],
                  in0=onesr_sb[:], scalar1=wsc[:, 3:4], scalar2=None,
                  op0=ALU.mult)
        p.wait("sync", t3)
        d3 = p.dma("sync", (wrec or g_wrec)[1][:], wrow[:], "sX")
        p.wait("vector", d3)
        return d3

    def w_quant_half(WT_slice, wq_slice, wq_free_tok, wr):
        si = stg_i[0]
        stg_pp.guard("sync", si)
        lt = p.dma("sync", stgw[si % 2][:], WT_slice,
                   "sSg0" if si % 2 == 0 else "sSg1")
        p.wait("vector", lt)
        if wq_free_tok is not None:
            p.wait("vector", wq_free_tok)
        p.op("vector", "tensor_scalar", out=wq_slice, in0=stgw[si % 2][:],
             scalar1=wr[0][:], scalar2=0.5, op0=ALU.mult, op1=ALU.is_gt)
        p.op("vector", "tensor_scalar", out=wqmV[:], in0=stgw[si % 2][:],
             scalar1=wr[1][:], scalar2=0.5, op0=ALU.mult, op1=ALU.is_gt)
        qtok = p.op("vector", "tensor_tensor", sem="sV", out=wq_slice,
                    in0=wq_slice, in1=wqmV[:], op=ALU.subtract)
        stg_pp.done(si, qtok)
        stg_i[0] += 1
        return qtok

    xtt = [p.sbuf(f"xtt{i}", [128, DM], F32) for i in range(2)]
    x1t = [p.sbuf(f"x1t{i}", [128, DM], F32) for i in range(2)]
    x2t = [p.sbuf(f"x2t{i}", [128, DM], F32) for i in range(2)]
    a32t = p.sbuf("a32t", [128, DM], F32)
    a52t = p.sbuf("a52t", [128, DM], F32)
    m32t = p.sbuf("m32t", [128, DM], F32)
    ddt = p.sbuf("ddt", [128, DM], F32)
    xq_t = p.sbuf("xq_t", [128, DM], F32)
    lnscr = p.sbuf("lnscr", [128, DM], F32)

    pA_free = {}
    pA_seq = 0
    pT_free = {}
    pT_seq = 0
    xtt_pp = PP(p, 2)
    x2t_pp = PP(p, 2)
    mv_prev = None
    xq_read = None
    aa_prev = None
    mmA_last = None
    for tt in range(NT):
        xtt_pp.guard("sync", tt)
        if tt == 0:
            p.wait("sync", y2_done)
            p.wait("sync", wout_ready)
        xld = p.dma("sync", xtt[tt % 2][:], x_tok[128 * tt:128 * (tt + 1), :],
                    "sXt0" if tt % 2 == 0 else "sXt1")
        x1_last = None
        for nch in range(2):
            p.wait("tensor", pA_free.get(pA_seq - 2))
            if pA_seq < 2:
                p.wait("tensor", io.get("psum_prev", [None, None])[pA_seq])
            if pA_seq == 0:
                p.wait("tensor", y2_done)
                p.wait("tensor", wout_ready)
            tok = None
            for k in range(16):
                tok = p.op("tensor", "matmul", sem="sT" if k == 15 else None,
                           out=pA[pA_seq % 2][:, 0:512],
                           lhsT=y2mm(k, tt),
                           rhs=wout(k)[:, 512 * nch:512 * (nch + 1)],
                           start=(k == 0), stop=(k == 15))
            mmA_last = tok
            p.wait("vector", tok)
            p.wait("vector", xld)
            x1_last = p.op("vector", "tensor_tensor", sem="sV",
                           out=x1t[tt % 2][:, 512 * nch:512 * (nch + 1)],
                           in0=pA[pA_seq % 2][:, 0:512],
                           in1=xtt[tt % 2][:, 512 * nch:512 * (nch + 1)],
                           op=ALU.add)
            pA_free[pA_seq] = x1_last
            pA_seq += 1
        xtt_pp.done(tt, x1_last)
        p.wait("scalar", x1_last)
        if mv_prev is not None:
            p.wait("scalar", mv_prev)
        p.op("scalar", "activation", out=lnscr[:], in_=x1t[tt % 2][:],
             func=AF.Identity, bias=0.0, scale=1.0, accum_out=lnsum[:, 0:1])
        sq_tok = p.op("scalar", "activation", sem="sS", out=lnscr[:],
                      in_=x1t[tt % 2][:], func=AF.Square, bias=0.0, scale=1.0,
                      accum_out=lnsum[:, 1:2])
        p.wait("vector", sq_tok)
        p.op("vector", "tensor_scalar", out=mv[:, 0:1], in0=lnsum[:, 0:1],
             scalar1=1.0 / DM, scalar2=None, op0=ALU.mult)
        p.op("vector", "scalar_tensor_tensor", out=lnsum[:, 2:3],
             in0=mv[:, 0:1], scalar=mv[:, 0:1], in1=lnsum[:, 1:2],
             op0=ALU.mult, op1=ALU.bypass)
        mv_tok = p.op("vector", "scalar_tensor_tensor", sem="sV",
                      out=mv[:, 1:2], in0=lnsum[:, 1:2], scalar=1.0 / DM,
                      in1=lnsum[:, 2:3], op0=ALU.mult, op1=ALU.subtract)
        mv_prev = mv_tok
        p.wait("scalar", mv_tok)
        p.op("scalar", "activation", out=rln[:], in_=mv[:, 1:2], func=AF.Ln,
             bias=cst_sb[:, 2:3], scale=1.0)
        r_tok = p.op("scalar", "activation", sem="sS", out=rln[:], in_=rln[:],
                     func=AF.Exp, bias=0.0, scale=-0.5)
        p.wait("vector", r_tok)
        x2t_pp.guard("vector", tt)
        x2_tok = p.op("vector", "tensor_scalar", sem="sV",
                      out=x2t[tt % 2][:], in0=x1t[tt % 2][:],
                      scalar1=mv[:, 0:1], scalar2=rln[:], op0=ALU.subtract,
                      op1=ALU.mult)
        p.wait("gpsimd", x2_tok)
        x2st = p.dma("gpsimd", x2_sp[128 * tt:128 * (tt + 1), :],
                     (x1t if DBG_X1 else x2t)[tt % 2][:],
                     "sX2a" if tt % 2 == 0 else "sX2b")
        p.op("vector", "tensor_reduce", out=xs_t[:], in_=x2t[tt % 2][:], axis=AX,
             op=ALU.max, apply_absolute_value=True)
        p.op("vector", "tensor_scalar", out=xs_t[:], in0=xs_t[:],
             scalar1=EPS_Q, scalar2=None, op0=ALU.max)
        p.op("vector", "tensor_copy", out=xs_all[:, tt:tt + 1], in_=xs_t[:])
        p.op("vector", "reciprocal", out=xs_t[:], in_=xs_t[:])
        qsc_tok = p.op("vector", "tensor_scalar", sem="sV", out=xs_t[:],
                       in0=xs_t[:], scalar1=QB, scalar2=None, op0=ALU.mult)
        if tt == 0:
            p.wait("vector", qg_tok)
        p.wait("scalar", qsc_tok)
        if aa_prev is not None:
            p.wait("scalar", aa_prev)
        p.op("scalar", "activation", out=a32t[:], in_=x2t[tt % 2][:],
             func=AF.Identity, bias=cst_sb[:, 0:1], scale=xs_t[:])
        p.op("scalar", "activation", out=a52t[:],
             in_=x2t[tt % 2][:], func=AF.Identity,
             bias=cst_sb[:, 1:2], scale=xs_t[:])
        aa_tok = p.op("scalar", "activation", sem="sS", out=m32t[:],
                      in_=a32t[:], func=AF.Identity, bias=cst_sb[:, 3:4],
                      scale=1.0)
        p.wait("vector", aa_tok)
        if xq_read is not None:
            p.wait("vector", xq_read)
        p.op("vector", "scalar_tensor_tensor", out=ddt[:], in0=a52t[:],
             scalar=C52, in1=m32t[:], op0=ALU.subtract, op1=ALU.subtract)
        if tt == 0:
            p.wait("vector", qg_tok)
        xq_tok = p.op("vector", "scalar_tensor_tensor", sem="sV",
                      out=xq_t[:], in0=ddt[:], scalar=qg[:],
                      in1=m32t[:], op0=ALU.mult, op1=ALU.add)
        aa_prev = xq_tok
        x2t_pp.done(tt, x2st)
        x2st_last = x2st
        tr_last = None
        for jq in range(2):
            for j4 in range(4):
                j = 4 * jq + j4
                p.wait("tensor", xq_tok)
                if j4 == 0:
                    p.wait("tensor", pT_free.get(pT_seq - 2))
                if pT_seq == 0 and j4 == 0:
                    p.wait("tensor", tok_idf)
                tr = p.op("tensor", "transpose",
                          sem="sT" if j4 == 3 else None,
                          out=pT_[pT_seq % 2][:, 128 * j4:128 * (j4 + 1)],
                          in_=xq_t[:, 128 * j:128 * (j + 1)],
                          identity=idf_sb[:])
            tr_last = tr
            p.wait("vector", tr)
            for j4 in range(4):
                j = 4 * jq + j4
                pT_free[pT_seq] = p.op(
                    "vector", "tensor_copy",
                    sem="sV" if j4 == 3 else None,
                    out=xqf[j][:, 128 * tt:128 * (tt + 1)],
                    in_=pT_[pT_seq % 2][:, 128 * j4:128 * (j4 + 1)])
            pT_seq += 1
        xq_read = tr_last
        for ii in range(4):
            i = 4 * tt + ii
            if i < 32:
                wmean_step(W1T[128 * (i // 4):128 * (i // 4) + 128,
                               1024 * (i % 4):1024 * (i % 4) + 1024])
            else:
                if i == 32:
                    w1sc = w_scales(wmean_state["rtok"], w1s)
                    wmean_state["first"] = True
                    wmean_state["wsum_guard"] = w1sc_trt[0]
                wmean_step(W2T[128 * (i - 32):128 * (i - 31), :])
    xqf_ready = pT_free[pT_seq - 1]
    w2sc = w_scales(wmean_state["rtok"], w2s, wrec=w2rec)

    w1q_tok = {}
    w2q_tok = {}
    w1_ready = None
    first_wq = mmA_last
    for h in range(4):
        for k in range(8):
            w1_ready = w_quant_half(
                W1T[128 * k:128 * (k + 1), 1024 * h:1024 * (h + 1)],
                segs[4 * k + h][:, 0:1024], first_wq, g_wrec)
            first_wq = None
            w1q_tok[(k, h)] = w1_ready
    p.phase_end()

    w2_lts = {}
    w2stg_free = {}

    def w2_load(i):
        if i >= 32:
            return
        p.wait("sync", w2stg_free.get(i - 2))
        w2_lts[i] = p.dma("sync", wstg[i % 2][:],
                          W2T[128 * i:128 * (i + 1), :],
                          "sTw0" if i % 2 == 0 else "sTw1")

    def w2_quant(i, extra=()):
        p.wait("vector", w2_lts[i])
        for t in extra:
            p.wait("vector", t)
        dst = w2q_ap(i, 0, 1024)
        p.op("vector", "tensor_scalar", out=dst, in0=wstg[i % 2][:],
             scalar1=w2rec[0][:], scalar2=0.5, op0=ALU.mult, op1=ALU.is_gt)
        p.op("vector", "tensor_scalar", out=wqmV[:], in0=wstg[i % 2][:],
             scalar1=w2rec[1][:], scalar2=0.5, op0=ALU.mult, op1=ALU.is_gt)
        qt = p.op("vector", "tensor_tensor", sem="sV",
                  out=dst, in0=dst, in1=wqmV[:], op=ALU.subtract)
        w2q_tok[i] = qt
        w2stg_free[i] = qt
        w2_load(i + 2)
        return qt

    p.phase_begin()
    b1b_sb = p.sbuf("b1b_sb", [128, 4 * DM], F32)
    p.wait("sync", w1_ready)
    p.wait("sync", p.total("sX2a"))
    p.wait("sync", p.total("sX2b"))
    p.wait("sync", xqf_ready)
    b1b_tok = p.dma("sync", b1b_sb[:], b1b, "sCst")
    w2_load(0)
    w2_load(1)
    xq2g = [p.sbuf("xq2g0", [128, 4 * DM], BF16)] * 2
    h_t = p.sbuf("h_t", [128, 4 * DM], F32)
    a32h = p.sbuf("a32h", [128, 1024], F32)
    a52h = p.sbuf("a52h", [128, 1024], F32)
    xq2q = [p.sbuf(f"xq2q{i}", [128, 1024], F32) for i in range(2)]

    xq2f_pp = PP(p, 1)
    xq2q_free = {}
    f15_tok = {}
    gelu_last = None
    ah_prev = None
    xq2_read = None
    mmF_last = None
    sp_last = None
    hfree_prev = None
    for tt in range(NT):
        s1_tok = p.op("vector", "scalar_tensor_tensor", sem="sV", out=s1_t[:],
                      in0=xs_all[:, tt:tt + 1], scalar=INV_QB, in1=w1s[:],
                      op0=ALU.mult, op1=ALU.mult)
        for hch in range(8):
            p.wait("tensor", pA_free.get(pA_seq - 2))
            if pA_seq == 32:
                p.wait("tensor", xqf_ready)
            tok = None
            for k in range(8):
                if tt == 0:
                    p.wait("tensor", w1q_tok[(k, hch // 2)])
                tok = p.op("tensor", "matmul", sem="sT" if k == 7 else None,
                           out=pA[pA_seq % 2][:, 0:512],
                           lhsT=xqf[k][:, 128 * tt:128 * (tt + 1)],
                           rhs=segs[4 * k + hch // 2][:, 512 * (hch % 2):
                                    512 * (hch % 2) + 512],
                           start=(k == 0), stop=(k == 7))
            mmF_last = tok
            if tt == NT - 1:
                f15_tok[hch] = tok
            p.wait("vector", tok)
            p.wait("vector", s1_tok)
            if pA_seq == 32:
                p.wait("vector", b1b_tok)
            if hch == 0 and hfree_prev is not None:
                p.wait("vector", hfree_prev)
            z_tok = p.op("vector", "scalar_tensor_tensor", sem="sV",
                         out=h_t[:, 512 * hch:512 * (hch + 1)],
                         in0=pA[pA_seq % 2][:, 0:512],
                         scalar=s1_t[:],
                         in1=b1b_sb[:, 512 * hch:512 * (hch + 1)],
                         op0=ALU.mult, op1=ALU.add)
            pA_free[pA_seq] = z_tok
            p.wait("scalar", z_tok)
            gelu_last = p.op("scalar", "activation", sem="sS",
                             out=h_t[:, 512 * hch:512 * (hch + 1)],
                             in_=h_t[:, 512 * hch:512 * (hch + 1)],
                             func=AF.Gelu, bias=0.0, scale=1.0)
            pA_seq += 1
        p.wait("vector", gelu_last)
        p.op("vector", "tensor_reduce", out=hs_t[:], in_=h_t[:], axis=AX,
             op=ALU.max, apply_absolute_value=True)
        p.op("vector", "tensor_scalar", out=hs_t[:], in0=hs_t[:],
             scalar1=EPS_Q, scalar2=None, op0=ALU.max)
        if tt == 0:
            p.wait("vector", w2sc)
        p.op("vector", "scalar_tensor_tensor", out=s2_all[:, tt:tt + 1],
             in0=hs_t[:], scalar=INV_QB, in1=w2s[:], op0=ALU.mult, op1=ALU.mult)
        p.op("vector", "reciprocal", out=hs_t[:], in_=hs_t[:])
        q2_tok = p.op("vector", "tensor_scalar", sem="sV", out=hs_t[:],
                      in0=hs_t[:], scalar1=QB, scalar2=None, op0=ALU.mult)
        xq2_tok = None
        hfree = None
        tr_last = None
        cp_tok = None
        xq2f_pp.guard("vector", tt)
        for qh in range(4):
            sl = slice(1024 * qh, 1024 * (qh + 1))
            p.wait("scalar", q2_tok)
            if ah_prev is not None:
                p.wait("scalar", ah_prev)
            p.op("scalar", "activation", out=a32h[:], in_=h_t[:, sl],
                 func=AF.Identity, bias=cst_sb[:, 0:1], scale=hs_t[:])
            p.op("scalar", "activation", out=a52h[:],
                 in_=h_t[:, sl], func=AF.Identity, bias=cst_sb[:, 1:2],
                 scale=hs_t[:])
            ah_tok = p.op("scalar", "activation", sem="sS", out=a32h[:],
                          in_=a32h[:], func=AF.Identity, bias=cst_sb[:, 3:4],
                          scale=1.0)
            p.wait("vector", ah_tok)
            p.wait("vector", xq2q_free.get((tt, qh)))
            p.op("vector", "scalar_tensor_tensor", out=a52h[:], in0=a52h[:],
                 scalar=C52, in1=a32h[:], op0=ALU.subtract, op1=ALU.subtract)
            xq2_tok = p.op("vector", "scalar_tensor_tensor", sem="sV",
                           out=xq2q[qh % 2][:], in0=a52h[:], scalar=qg[:],
                           in1=a32h[:], op0=ALU.mult, op1=ALU.add)
            ah_prev = xq2_tok
            hfree = xq2_tok
            for jq2 in range(2):
                for j4 in range(4):
                    j = 4 * jq2 + j4
                    p.wait("tensor", xq2_tok)
                    if j4 == 0:
                        p.wait("tensor", pT_free.get(pT_seq - 2))
                    tr = p.op("tensor", "transpose",
                              sem="sT" if j4 == 3 else None,
                              out=pT_[pT_seq % 2][:, 128 * j4:128 * (j4 + 1)],
                              in_=xq2q[qh % 2][:, 128 * j:128 * (j + 1)],
                              identity=idf_sb[:])
                tr_last = tr
                p.wait("vector", tr)
                cp_tok = p.op("vector", "tensor_copy", sem="sV",
                              out=xq2g[tt % 2][:, 1024 * qh + 512 * jq2:
                                               1024 * qh + 512 * (jq2 + 1)],
                              in_=pT_[pT_seq % 2][:, 0:512])
                pT_free[pT_seq] = cp_tok
                pT_seq += 1
            nq = qh + 2
            xq2q_free[(tt, nq) if nq < 4 else (tt + 1, nq - 4)] = tr_last
        hfree_prev = hfree
        w2_quant(tt, extra=(mmA_last, w2sc) if tt == 0 else ())
        p.wait("sync", cp_tok)
        sp_last = p.dma("sync", xq2_sp[128 * tt:128 * (tt + 1), :],
                        xq2g[tt % 2][:], "sSp0" if tt % 2 == 0 else "sSp1")
        xq2f_pp.done(tt, sp_last)
        xq2_read = tr_last

    for i in range(16, 32):
        m = i - 16
        w2_quant(i, extra=(f15_tok[2 * (m % 4) + 1],))

    p.phase_end()

    p.phase_begin()
    x2r = [p.sbuf(f"x2r{i}", [128, DM], F32) for i in range(2)]
    t2t = [p.sbuf(f"t2t{i}", [128, 512], F32) for i in range(2)]
    x3p = [p.sbuf(f"x3p{i}", [128, DM], F32) for i in range(2)]
    x3t = [p.sbuf(f"x3t{i}", [128, DM], F32) for i in range(2)]
    lnscr2 = p.sbuf("lnscr2", [128, DM], F32)
    xq2lt = [p.sbuf(f"xq2lt{i}", [128, 4 * DM], BF16) for i in range(2)]
    xq2l_pp = PP(p, 2)
    x2r_pp = PP(p, 2)
    t2_pp = PP(p, 2)
    x3t_pp = PP(p, 2)
    mv_prev2 = None
    for tt in range(NT):
        xq2l_pp.guard("sync", tt)
        if tt == 0:
            p.wait("sync", p.total("sSp0"))
            p.wait("sync", p.total("sSp1"))
            p.wait("sync", p.total("sX2a"))
            p.wait("sync", p.total("sX2b"))
            p.wait("sync", xq2_read)
        lq = p.dma("sync", xq2lt[tt % 2][:],
                   xq2_sp[128 * tt:128 * (tt + 1), :],
                   "sQl0" if tt % 2 == 0 else "sQl1")
        x2r_pp.guard("sync", tt)
        lx = p.dma("sync", x2r[tt % 2][:], x2_sp[128 * tt:128 * (tt + 1), :],
                   "sXr0" if tt % 2 == 0 else "sXr1")
        ev_last = None
        mm_last = None
        for nch in range(2):
            p.wait("tensor", pA_free.get(pA_seq - 2))
            p.wait("tensor", lq)
            tok = None
            for k in range(32):
                if tt == 0:
                    p.wait("tensor", w2q_tok[k])
                tok = p.op("tensor", "matmul", sem="sT" if k == 31 else None,
                           out=pA[pA_seq % 2][:, 0:512],
                           lhsT=xq2lt[tt % 2][:, 128 * k:128 * (k + 1)],
                           rhs=w2q_ap(k, 512 * nch, 512 * (nch + 1)),
                           start=(k == 0), stop=(k == 31))
            mm_last = tok
            p.wait("vector", tok)
            p.wait("vector", lx)
            if tt == 0 and nch == 0:
                p.wait("vector", tok_b)
            t2_pp.guard("vector", pA_seq)
            p.op("vector", "scalar_tensor_tensor",
                 out=t2t[pA_seq % 2][:], in0=pA[pA_seq % 2][:, 0:512],
                 scalar=s2_all[:, tt:tt + 1],
                 in1=b2b_sb[:, 512 * nch:512 * (nch + 1)],
                 op0=ALU.mult, op1=ALU.add)
            ev_last = p.op("vector", "tensor_tensor", sem="sV",
                           out=x3p[tt % 2][:, 512 * nch:512 * (nch + 1)],
                           in0=t2t[pA_seq % 2][:],
                           in1=x2r[tt % 2][:, 512 * nch:512 * (nch + 1)],
                           op=ALU.add)
            pA_free[pA_seq] = ev_last
            t2_pp.done(pA_seq, ev_last)
            pA_seq += 1
        xq2l_pp.done(tt, mm_last)
        x2r_pp.done(tt, ev_last)
        p.wait("scalar", ev_last)
        if mv_prev2 is not None:
            p.wait("scalar", mv_prev2)
        p.op("scalar", "activation", out=lnscr2[:], in_=x3p[tt % 2][:],
             func=AF.Identity, bias=0.0, scale=1.0, accum_out=lnsum[:, 0:1])
        sq_tok = p.op("scalar", "activation", sem="sS", out=lnscr2[:],
                      in_=x3p[tt % 2][:], func=AF.Square, bias=0.0, scale=1.0,
                      accum_out=lnsum[:, 1:2])
        p.wait("vector", sq_tok)
        p.op("vector", "tensor_scalar", out=mv[:, 0:1], in0=lnsum[:, 0:1],
             scalar1=1.0 / DM, scalar2=None, op0=ALU.mult)
        p.op("vector", "scalar_tensor_tensor", out=lnsum[:, 2:3],
             in0=mv[:, 0:1], scalar=mv[:, 0:1], in1=lnsum[:, 1:2],
             op0=ALU.mult, op1=ALU.bypass)
        mv_tok = p.op("vector", "scalar_tensor_tensor", sem="sV",
                      out=mv[:, 1:2], in0=lnsum[:, 1:2], scalar=1.0 / DM,
                      in1=lnsum[:, 2:3], op0=ALU.mult, op1=ALU.subtract)
        mv_prev2 = mv_tok
        p.wait("scalar", mv_tok)
        p.op("scalar", "activation", out=rln[:], in_=mv[:, 1:2], func=AF.Ln,
             bias=cst_sb[:, 2:3], scale=1.0)
        r_tok = p.op("scalar", "activation", sem="sS", out=rln[:], in_=rln[:],
                     func=AF.Exp, bias=0.0, scale=-0.5)
        p.wait("vector", r_tok)
        x3t_pp.guard("vector", tt)
        x3_tok = p.op("vector", "tensor_scalar", sem="sV",
                      out=x3t[tt % 2][:], in0=x3p[tt % 2][:],
                      scalar1=mv[:, 0:1], scalar2=rln[:], op0=ALU.subtract,
                      op1=ALU.mult)
        p.wait("gpsimd", x3_tok)
        ost = p.dma("gpsimd", out_t[128 * tt:128 * (tt + 1), :],
                    x3t[tt % 2][:], "sOt0" if tt % 2 == 0 else "sOt1")
        x3t_pp.done(tt, ost)
    p.phase_end()

    p.phase_end()
    return {"outs": [p.total(nm) for nm in
                     ("sOt0", "sOt1", "sX2a", "sX2b", "sSp0", "sSp1")]}


def build_P3():
    nc = bass.Bass("TRN2", target_bir_lowering=False)
    p = Prog(nc)
    io = {
        "x_tok": nc.dram_tensor("x_tok", [TLOC, DM], F32, kind="ExternalInput").ap(),
        "ya_o": nc.dram_tensor("y_tok", [4, 128, 2, 8, 512], F32, kind="ExternalInput").ap(),
        "u_in": nc.dram_tensor("u_in", [DI, TLOC], F32, kind="ExternalInput").ap(),
        "g_in": nc.dram_tensor("g_in", [DI, TLOC], F32, kind="ExternalInput").ap(),
        "W_outT": nc.dram_tensor("W_outT", [DI, DM], F32, kind="ExternalInput").ap(),
        "W1T": nc.dram_tensor("W1T", [DM, 4 * DM], F32, kind="ExternalInput").ap(),
        "W2T": nc.dram_tensor("W2T", [4 * DM, DM], F32, kind="ExternalInput").ap(),
        "b1b": nc.dram_tensor("b1b", [128, 4 * DM], F32, kind="ExternalInput").ap(),
        "b2b": nc.dram_tensor("b2b", [128, DM], F32, kind="ExternalInput").ap(),
        "d_p": nc.dram_tensor("d_p", [128, 16], F32, kind="ExternalInput").ap(),
        "idb": nc.dram_tensor("idb", [128, 128], BF16, kind="ExternalInput").ap(),
        "cst": nc.dram_tensor("cst", [128, 4], F32, kind="ExternalInput").ap(),
        "idf": nc.dram_tensor("idf", [128, 128], F32, kind="ExternalInput").ap(),
        "onesr": nc.dram_tensor("onesr", [1, 128], F32, kind="ExternalInput").ap(),
        "out_t": nc.dram_tensor("out_t", [TLOC, DM], F32, kind="ExternalOutput").ap(),
        "x2_sp": nc.dram_tensor("x2_sp", [TLOC, DM], F32, kind="ExternalOutput").ap(),
        "xq2_sp": nc.dram_tensor("xq2_sp", [TLOC, 4 * DM], BF16, kind="ExternalOutput").ap(),
        "pA": [nc.psum_tensor(f"pA{i}", [128, 512], F32).__enter__() for i in range(2)],
        "pT_": [nc.psum_tensor(f"pT{i}", [128, 512], F32).__enter__() for i in range(2)],
        "pR": nc.psum_tensor("pR", [1, 512], F32).__enter__(),
    }
    r = emit_P3(p, io)
    for t in r["outs"]:
        p.wait("gpsimd", t)
    p.build_block()
    return nc





def build_fused():
    nc = bass.Bass("TRN2", target_bir_lowering=False)
    p = Prog(nc)
    EI, EO, IN = "ExternalInput", "ExternalOutput", "Internal"
    d = nc.dram_tensor
    io1 = {
        "xT": d("xT", [DM, TEXT], F32, kind=EI).ap(),
        "W_inT": d("W_inT", [DM, 2 * DI], F32, kind=EI).ap(),
        "W_dtT": d("W_dtT", [DI, DI], F32, kind=EI).ap(),
        "cw": d("cw", [128, 16 * KCONV], F32, kind=EI).ap(),
        "cb": d("cb", [128, 16], F32, kind=EI).ap(),
        "bdt": d("bdt", [128, 16], F32, kind=EI).ap(),
        "dtb": d("dtb", [128, 16], F32, kind=EI).ap(),
        "u_out": d("u_sp", [DI, TLOC], F32, kind=IN).ap(),
        "g_out": d("g_sp", [DI, TLOC], F32, kind=IN).ap(),
        "dt_out": d("dta_i", [4, 8, DSL, 512], F32, kind=IN).ap(),
        "dtu_out": d("dtua_i", [4, 8, DSL, 512], F32, kind=IN).ap(),
    }
    io2 = {
        "dta_o": d("dta_o", [4, 128, 2, 8, 512], F32, kind=IN).ap(),
        "dtua_o": d("dtua_o", [4, 128, 2, 8, 512], F32, kind=IN).ap(),
        "alog": d("alog", [128, 2 * NST], F32, kind=EI).ap(),
        "bp": d("bp", [128, 2 * NST], F32, kind=EI).ap(),
        "cp": d("cp", [128, 2 * NST], F32, kind=EI).ap(),
        "ya_in": d("ya_i", [4, 8, DSL, 512], F32, kind=IN).ap(),
    }
    io3 = {
        "x_tok": d("x_tok", [TLOC, DM], F32, kind=EI).ap(),
        "ya_o": d("ya_o", [4, 128, 2, 8, 512], F32, kind=IN).ap(),
        "u_in": io1["u_out"], "g_in": io1["g_out"],
        "W_outT": d("W_outT", [DI, DM], F32, kind=EI).ap(),
        "W1T": d("W1T", [DM, 4 * DM], F32, kind=EI).ap(),
        "W2T": d("W2T", [4 * DM, DM], F32, kind=EI).ap(),
        "b1b": d("b1b", [128, 4 * DM], F32, kind=EI).ap(),
        "b2b": d("b2b", [128, DM], F32, kind=EI).ap(),
        "d_p": d("d_p", [128, 16], F32, kind=EI).ap(),
        "idb": d("idb", [128, 128], BF16, kind=EI).ap(),
        "cst": d("cst", [128, 4], F32, kind=EI).ap(),
        "idf": d("idf", [128, 128], F32, kind=EI).ap(),
        "onesr": d("onesr", [1, 128], F32, kind=EI).ap(),
        "out_t": d("out_t", [TLOC, DM], F32, kind=EO).ap(),
        "x2_sp": d("x2_sp", [TLOC, DM], F32, kind=IN).ap(),
        "xq2_sp": d("xq2_sp", [TLOC, 4 * DM], BF16, kind=IN).ap(),
    }
    idf_shared = io3["idf"]
    io2["idf"] = idf_shared

    psA = [p.psum(f"psA{i}", [128, 512]) for i in range(2)]
    psB = [p.psum(f"psB{i}", [128, 512]) for i in range(2)]
    pR = p.psum("pR", [1, 512])
    io1["pm"] = psA
    io2["pY"] = psA
    io3["pA"] = psA
    io3["pT_"] = psB
    io3["pR"] = pR

    p.prefix = "A_"
    r1 = emit_P1(p, io1)

    p.prefix = "X_"
    rsem_dt = p.new_sem("sRdt")
    rsem_du = p.new_sem("sRdu")
    rsem_y = p.new_sem("sRy")
    lsem = p.new_sem("sLoc")
    NTRIG = 4
    TW = TLOC // NTRIG

    def a2a_trigger(src_ap, dst_ap, n, rsem):
        src = src_ap[n].rearrange("s (dh dl) t -> s dh dl t", dh=2)
        dst = dst_ap[n]
        p.q["gpsimd"].append(
            lambda e, src=src, dst=dst: e.remote_dma_broadcast_einops(
                dst, src, p.sem[rsem], local_sem=p.sem[lsem],
                rearrange="R S dh dl t -> S dl dh R t"))
        p.cnt[rsem] += 16

    for t in r1["dtdu"]:
        p.wait("gpsimd", t)
    for n in range(NTRIG):
        a2a_trigger(io1["dt_out"], io2["dta_o"], n, rsem_dt)
        a2a_trigger(io1["dtu_out"], io2["dtua_o"], n, rsem_du)
    rs1a = (rsem_dt, p.cnt[rsem_dt])
    rs1b = (rsem_du, p.cnt[rsem_du])

    p.prefix = "B_"
    io2["entry"] = rs1a
    io2["entry2"] = rs1b
    io2["psum_prev"] = r1["psum_free"]
    r2 = emit_P2(p, io2)

    p.prefix = "X_"
    for t in r2["ystores"]:
        p.wait("gpsimd", t)
    for n in range(NTRIG):
        a2a_trigger(io2["ya_in"], io3["ya_o"], n, rsem_y)
    rs2 = (rsem_y, p.cnt[rsem_y])

    p.prefix = "C_"
    io3["entry"] = rs2
    io3["ug_stores"] = r1["ug"]
    io3["psum_prev"] = r2["psum_free"]
    r3 = emit_P3(p, io3)
    for t in r3["outs"]:
        p.wait("gpsimd", t)
    p.build_block()
    return nc



def kernel(L0_attn_hidden_states, L0_attn_W_in, L0_attn_W_out, L0_attn_W_dt,
           L0_attn_b_dt, L0_attn_conv_w, L0_attn_conv_b, L0_attn_A_log,
           L0_attn_B_param, L0_attn_C_param, L0_attn_D_param, L0_attn_dt_bias,
           L0_ffn_W1, L0_ffn_b1, L0_ffn_W2, L0_ffn_b2):
    f32 = np.float32
    x = np.asarray(L0_attn_hidden_states, f32)
    xf = np.ascontiguousarray(x.reshape(BATCH * L, DM))

    def colmajor16(v):
        return np.ascontiguousarray(np.asarray(v, f32).reshape(16, 128).T)

    W_inT = np.ascontiguousarray(np.asarray(L0_attn_W_in, f32).T)
    W_dtT = np.ascontiguousarray(np.asarray(L0_attn_W_dt, f32).T)
    W_outT = np.ascontiguousarray(np.asarray(L0_attn_W_out, f32).T)
    W1T = np.ascontiguousarray(np.asarray(L0_ffn_W1, f32).T)
    W2T = np.ascontiguousarray(np.asarray(L0_ffn_W2, f32).T)
    cw = np.ascontiguousarray(
        np.asarray(L0_attn_conv_w, f32).reshape(16, 128, KCONV)
        .transpose(1, 0, 2).reshape(128, 16 * KCONV))
    cb = colmajor16(L0_attn_conv_b)
    bdt = colmajor16(L0_attn_b_dt)
    dtb = colmajor16(L0_attn_dt_bias)
    d_p = colmajor16(L0_attn_D_param)
    b1b = np.ascontiguousarray(
        np.broadcast_to(np.asarray(L0_ffn_b1, f32), (128, 4 * DM)))
    b2b = np.ascontiguousarray(
        np.broadcast_to(np.asarray(L0_ffn_b2, f32), (128, DM)))
    idf = np.eye(128, dtype=f32)
    idb = np.eye(128, dtype=BF)
    cst = np.ascontiguousarray(np.broadcast_to(
        np.array([C23, C52, EPS_LN, -C23], f32), (128, 4)))

    ncA = build_P1()
    in_maps = []
    for c in range(NCORES):
        t0 = c * TLOC
        if t0 % L == 0:
            halo = np.zeros((HALO, DM), f32)
        else:
            halo = xf[t0 - HALO:t0]
        x_ext = np.concatenate([halo, xf[t0:t0 + TLOC]], axis=0)
        in_maps.append({
            "xT": np.ascontiguousarray(x_ext.T),
            "W_inT": W_inT, "W_dtT": W_dtT,
            "cw": cw, "cb": cb, "bdt": bdt, "dtb": dtb,
        })
    resA = run_bass_kernel_spmd(ncA, in_maps, core_ids=list(range(NCORES)))
    u_all = [np.asarray(r["u_out"]) for r in resA.results]
    g_all = [np.asarray(r["g_out"]) for r in resA.results]
    dt_all = [np.asarray(r["dt_out"]) for r in resA.results]
    dtu_all = [np.asarray(r["dtu_out"]) for r in resA.results]

    def to_slices(parts):
        out = [np.empty((4, 128, 2, NCORES, 512), f32) for _ in range(NCORES)]
        for r in range(NCORES):
            pr = parts[r]
            for me in range(NCORES):
                blk = pr[:, me].reshape(4, 2, 128, 512)
                out[me][:, :, :, r, :] = blk.transpose(0, 2, 1, 3)
        return out

    dt_sl = to_slices(dt_all)
    dtu_sl = to_slices(dtu_all)

    def slice32(v):
        v = np.asarray(v, f32)
        return [np.ascontiguousarray(
            v[DSL * c:DSL * (c + 1)].reshape(2, 128, NST)
            .transpose(1, 0, 2).reshape(128, 2 * NST)) for c in range(NCORES)]

    al_sl = slice32(L0_attn_A_log)
    b_sl = slice32(L0_attn_B_param)
    c_sl = slice32(L0_attn_C_param)

    ncB = build_P2()
    in_maps = [{
        "dt_sl": dt_sl[c],
        "dtu_sl": dtu_sl[c],
        "alog": al_sl[c], "bp": b_sl[c], "cp": c_sl[c], "idf": idf,
    } for c in range(NCORES)]
    resB = run_bass_kernel_spmd(ncB, in_maps, core_ids=list(range(NCORES)))
    y_all = [np.asarray(r["y_sl"]) for r in resB.results]
    y_tok = to_slices(y_all)

    ncC = build_P3()
    in_maps = []
    for c in range(NCORES):
        t0 = c * TLOC
        in_maps.append({
            "x_tok": np.ascontiguousarray(xf[t0:t0 + TLOC]),
            "y_tok": np.ascontiguousarray(y_tok[c]),
            "u_in": u_all[c], "g_in": g_all[c],
            "W_outT": W_outT, "W1T": W1T, "W2T": W2T,
            "b1b": b1b, "b2b": b2b, "d_p": d_p, "idb": idb, "cst": cst,
            "idf": idf, "onesr": np.ones((1, 128), f32),
        })
    resC = run_bass_kernel_spmd(ncC, in_maps, core_ids=list(range(NCORES)))
    out = np.empty((BATCH * L, DM), f32)
    for c in range(NCORES):
        out[c * TLOC:(c + 1) * TLOC] = np.asarray(resC.results[c]["out_t"])
    return out.reshape(BATCH, L, DM)
